# version 04
# baseline (speedup 1.0000x reference)
import contextlib
import numpy as np
import ml_dtypes

import concourse.bass as bass
import concourse.mybir as mybir
from concourse.bass_utils import run_bass_kernel_spmd

F32 = mybir.dt.float32
BF16 = mybir.dt.bfloat16
I16 = mybir.dt.int16
I32 = mybir.dt.int32
U8 = mybir.dt.uint8
AF = mybir.ActivationFunctionType
ALU = mybir.AluOpType
AX = mybir.AxisListType

NC = 8
T = 2048
F = 2048
FCH = 16
TAIL = 1024
HID0 = 5632
I0 = 8192
MLP1 = 16384
EPS0 = 1e-5
EPSL = 1e-12
EPSR = 1.1920929e-07

bf = ml_dtypes.bfloat16



def make_pi():
    r = np.arange(F)
    c = r // 256
    j = r % 256
    return np.where(j < 128, 2 * (128 * c + j), 2 * (128 * c + j - 128) + 1)


def make_pi2():
    s = np.arange(MLP1)
    c = s // 2048
    j = s % 2048
    return np.where(j < 1024, 2 * (1024 * c + j), 2 * (1024 * c + j - 1024) + 1)


PI = make_pi()
PI2 = make_pi2()


def _u8view(a_int32):
    a = np.ascontiguousarray(a_int32)
    return a.view(np.uint8)[..., 0::4] if a.dtype.itemsize == 4 else a


def _vec(v):
    return np.ascontiguousarray(np.asarray(v, np.float32)[PI].reshape(FCH, 128).T)


def stage_inputs(inp):
    x = np.asarray(inp["L0_attn_x"], np.float32)
    per = []
    c = np.ascontiguousarray

    Wv = c(np.asarray(inp["L0_attn_Wv"], np.float32)[PI][:, PI])
    Wr = c(np.asarray(inp["L0_attn_Wr"], np.float32)[PI][:, PI])
    Wo = c(np.asarray(inp["L0_attn_Wo"], np.float32)[PI][:, PI])
    Wcr = c(np.asarray(inp["L0_attn_Wcr"], np.float32)[PI][:, PI])
    Wck = c(np.asarray(inp["L0_attn_Wck"], np.float32)[PI, :])
    Wcv = c(np.asarray(inp["L0_attn_Wcv"], np.float32)[:, PI])
    Wq = c(np.asarray(inp["L1_attn_Wq"], np.float32)[PI, :])
    Wv1 = c(np.asarray(inp["L1_attn_Wv"], np.float32)[PI, :])
    Wo1 = c(np.asarray(inp["L1_attn_Wo"], np.float32)[:, PI])

    w1T = np.asarray(inp["L0_ffn_w1"], np.float32).T[PI, :]
    w2T = np.asarray(inp["L0_ffn_w2"], np.float32).T[PI, :]
    w3T = np.asarray(inp["L0_ffn_w3"], np.float32).T[:, PI]

    wgT = _u8view(np.asarray(inp["L1_ffn_gate_W_packed"])).T[:, PI2]
    wuT = _u8view(np.asarray(inp["L1_ffn_up_W_packed"])).T[:, PI2]
    wdT = _u8view(np.asarray(inp["L1_ffn_down_W_packed"])).T[:, PI]
    gsT = np.asarray(inp["L1_ffn_gate_scales"], np.float32).T[:, PI2]
    usT = np.asarray(inp["L1_ffn_up_scales"], np.float32).T[:, PI2]
    dsT = np.asarray(inp["L1_ffn_down_scales"], np.float32).T[:, PI]

    vecs = {
        "ln1_g": _vec(inp["L0_attn_ln1_g"]), "ln1_b": _vec(inp["L0_attn_ln1_b"]),
        "ln2_g": _vec(inp["L0_attn_ln2_g"]), "ln2_b": _vec(inp["L0_attn_ln2_b"]),
        "tm_g": _vec(inp["L0_attn_tm_g"]), "tm_b": _vec(inp["L0_attn_tm_b"]),
        "cm_g": _vec(inp["L0_attn_cm_g"]), "cm_b": _vec(inp["L0_attn_cm_b"]),
        "tf": _vec(inp["L0_attn_time_first"]), "td": _vec(inp["L0_attn_time_decay"]),
        "bo": _vec(inp["L0_attn_bo"]),
    }
    tpos = np.arange(1023, -1, -1, dtype=np.float32).reshape(1, TAIL)
    ident = np.eye(128, dtype=bf)

    for core in range(NC):
        b, h = core // 2, core % 2
        xT = x[b].T[PI, :]
        t0 = h * T
        if h == 0:
            x0 = np.concatenate([np.zeros((F, 2), np.float32), xT[:, 0:T]], 1)
        else:
            x0 = xT[:, t0 - 2:t0 + T]
        m = {
            "x0": c(x0),
            "xtail": c(xT[:, 4096 - TAIL:]),
            "tpos": tpos,
            "maskm": np.full((128, 1), float(h == 1), np.float32),
            "ident": ident,
            "Wv": Wv, "Wr": Wr, "Wo": Wo, "Wcr": Wcr, "Wck": Wck, "Wcv": Wcv,
            "Wq": Wq, "Wv1": Wv1, "Wo1": Wo1,
            "w1T_s": c(w1T[256 * core:256 * (core + 1), :]),
            "w2T_s": c(w2T[256 * core:256 * (core + 1), :]),
            "w3T_s": c(w3T[704 * core:704 * (core + 1), :]),
            "wgT_s": c(wgT[128 * core:128 * (core + 1), :]),
            "wuT_s": c(wuT[128 * core:128 * (core + 1), :]),
            "wdT_s": c(wdT[1024 * core:1024 * (core + 1), :]),
            "gsT_s": c(gsT[8 * core:8 * (core + 1), :]),
            "usT_s": c(usT[8 * core:8 * (core + 1), :]),
            "dsT_s": c(dsT[64 * core:64 * (core + 1), :]),
        }
        m.update(vecs)
        per.append(m)
    return per


def assemble_output(shards):
    out = np.empty((4, 4096, 2048), np.float32)
    for core in range(NC):
        b, h = core // 2, core % 2
        blk = np.empty((T, F), np.float32)
        blk[:, PI] = shards[core].T
        out[b, h * T:(h + 1) * T, :] = blk
    return out



class Sem:
    __slots__ = ("h", "v", "name")

    def __init__(self, h, name):
        self.h, self.v, self.name = h, 0, name

    def inc(self, ins, n=1):
        ins.then_inc(self.h, n)
        self.v += n
        return self.v


class Prog:

    def __init__(self):
        self.nc = bass.Bass("TRN2", target_bir_lowering=False)
        self.es = contextlib.ExitStack()
        self.block = None
        self.sems = {}
        self._sb = {}
        self._consts = {}

    def constcol(self, val):
        key = float(val)
        if key not in self._consts:
            i = len(self._consts)
            col = self._cpool[:, i:i + 1]
            dv = self.sems["dv"]

            @self.block.vector
            def _(v):
                ins = v.memset(col, key)
                dv.inc(ins, 1)
            self._consts[key] = col
        return self._consts[key]

    def start_block(self):
        for name in ("sl", "sw", "ss", "pe", "dv", "ac", "po", "cc"):
            self.sems[name] = Sem(self.es.enter_context(self.nc.semaphore(name)), name)
        self._cpool = self.sbuf("const_pool", [128, 16], F32)
        self.ones_c = self.sbuf("g_ones_c", [128, 1], F32)
        self.ones_r = self.sbuf("g_ones_r", [1, 128], F32)
        self.block = self.es.enter_context(self.nc.Block())

        @self.block.vector
        def _(v):
            v.memset(self.ones_c[:], 1.0)
            ins = v.memset(self.ones_r[:], 1.0)
            self.sems["dv"].inc(ins, 1)

    _uid = 0

    def sbuf(self, name, shape, dtype, es=None):
        if name in self._sb:
            Prog._uid += 1
            name = f"{name}_{Prog._uid}"
        t = (es or self.es).enter_context(self.nc.sbuf_tensor(name, shape, dtype))
        self._sb[name] = t
        return t

    def fence(self):
        blk = self.block
        sems = [s for s in self.sems.values() if s.v]
        for eng in ("sync", "scalar", "vector", "tensor", "gpsimd"):
            @getattr(blk, eng)
            def _(q):
                for s in sems:
                    q.wait_ge(s.h, s.v)

    def psum(self, name, shape, dtype=F32, es=None):
        if name in self._sb:
            Prog._uid += 1
            name = f"{name}_{Prog._uid}"
        self._sb[name] = True
        return (es or self.es).enter_context(self.nc.psum_tensor(name, shape, dtype))

    def dram(self, name, shape, dtype, kind=None):
        if kind is None:
            return self.nc.dram_tensor(name, shape, dtype).ap()
        return self.nc.dram_tensor(name, shape, dtype, kind=kind).ap()

    def close(self):
        self.es.close()



NW1 = float(HID0 * F)


def emit_prep(P, io, gather):
    nc, blk = P.nc, P.block
    sl, sw, ss = P.sems["sl"], P.sems["sw"], P.sems["ss"]
    pe, dv, ac, po, ccs = (P.sems[k] for k in ("pe", "dv", "ac", "po", "cc"))

    es1 = contextlib.ExitStack()
    w1sb = P.sbuf("p_w1", [128, 2, HID0], F32, es1)
    w2sb = P.sbuf("p_w2", [128, 2, HID0], F32, es1)
    w3sb = P.sbuf("p_w3", [128, 6, F], F32, es1)
    red = P.sbuf("p_red", [128, 8], F32, es1)
    ones = P.sbuf("p_ones", [128, 1], F32, es1)
    gsb = P.sbuf("p_gsb", [128, 8], F32, es1)
    gal2 = P.sbuf("p_gal2", [128, 32], F32, es1)
    invg = P.sbuf("p_invg", [128, 4], F32, es1)
    qa = P.sbuf("p_qa", [128, HID0], F32, es1)
    qb = P.sbuf("p_qb", [128, HID0], F32, es1)
    qq = P.sbuf("p_qq", [128, HID0], BF16, es1)
    ps_g = P.psum("p_psg", [128, 8], F32, es1)

    @blk.sync
    def _(q):
        for ins in (
            q.dma_start(out=w1sb[:], in_=io["w1T_s"].rearrange("(c p) n -> p c n", p=128)),
            q.dma_start(out=w2sb[:], in_=io["w2T_s"].rearrange("(c p) n -> p c n", p=128)),
            q.dma_start(out=w3sb[:, 0:5, :],
                        in_=io["w3T_s"][0:640, :].rearrange("(c p) n -> p c n", p=128)),
            q.dma_start(out=w3sb[0:64, 5, :], in_=io["w3T_s"][640:704, :]),
        ):
            sl.inc(ins, 16)

    @blk.vector
    def _(v):
        v.memset(w3sb[64:128, 5, :], 0.0)
        v.memset(ones[:], 1.0)
        v.wait_ge(sl.h, sl.v)
        v.tensor_reduce(red[:, 0:1], w1sb[:].rearrange("p c n -> p (c n)"),
                        AX.X, ALU.add, apply_absolute_value=True)
        v.tensor_reduce(red[:, 1:2], w2sb[:].rearrange("p c n -> p (c n)"),
                        AX.X, ALU.add, apply_absolute_value=True)
        ins = v.tensor_reduce(red[:, 2:3], w3sb[:].rearrange("p c n -> p (c n)"),
                              AX.X, ALU.add, apply_absolute_value=True)
        v.memset(red[:, 3:4], 0.0)
        dv.inc(ins, 1)

    @blk.tensor
    def _(t):
        t.wait_ge(dv.h, dv.v)
        ins = t.matmul(ps_g[0:1, 0:4], ones[:], red[:, 0:4], start=True, stop=True)
        pe.inc(ins, 1)

    @blk.vector
    def _(v):
        v.wait_ge(pe.h, pe.v)
        ins = v.tensor_copy(gsb[0:1, 0:4], ps_g[0:1, 0:4])
        dv.inc(ins, 1)

    @blk.scalar
    def _(q):
        q.wait_ge(dv.h, dv.v)
        ins = q.dma_start(out=io["gpart"], in_=gsb[0:1, 0:4])
        ss.inc(ins, 16)

    @blk.gpsimd
    def _(g):
        g.wait_ge(ss.h, ss.v)
        ins = g.collective_compute(
            "AllGather", ALU.bypass, replica_groups=[list(range(NC))],
            ins=[io["gpart"].opt()], outs=[io["gall"].opt()])
        ccs.inc(ins, 1)
        g.wait_ge(ccs.h, ccs.v)
        ins = g.dma_start(
            out=gal2[0:1, 0:32],
            in_=io["gall"].rearrange("a b -> (a b)").unsqueeze(0))
        sw.inc(ins, 16)

    @blk.vector
    def _(v):
        v.wait_ge(sw.h, sw.v)
        ins = v.tensor_reduce(gsb[0:1, 4:8],
                              gal2[0:1, 0:32].rearrange("p (b a) -> p a b", a=4),
                              AX.X, ALU.add)
        dv.inc(ins, 1)
        v.wait_ge(dv.h, dv.v)
        ins = v.tensor_scalar(gsb[0:1, 0:4], gsb[0:1, 4:8], 1.0 / NW1, 1e-8,
                              ALU.mult, ALU.add)
        dv.inc(ins, 1)
        v.wait_ge(dv.h, dv.v)
        ins = v.reciprocal(gsb[0:1, 4:8], gsb[0:1, 0:4])
        dv.inc(ins, 1)
        v.wait_ge(dv.h, dv.v)
        ins = v.tensor_copy(gsb[0:1, 0:4], gsb[0:1, 4:8])
        dv.inc(ins, 1)

    @blk.scalar
    def _(q):
        q.wait_ge(dv.h, dv.v)
        ins = q.dma_start(out=io["ginv"], in_=gsb[0:1, 0:4])
        ss.inc(ins, 16)

    @blk.sync
    def _(q):
        q.wait_ge(ss.h, ss.v)
        ins = q.dma_start(out=invg[:], in_=io["ginv"].broadcast_to([128, 4]))
        sl.inc(ins, 16)

    def qchunk(src_c, col, rows, width, dst):
        @blk.vector
        def _(v):
            v.wait_ge(sl.h, sl.v)
            v.wait_ge(ss.h, ss.v)
            v.tensor_scalar(qa[:, 0:width], src_c, invg[:, col:col + 1], 0.5,
                            ALU.mult, ALU.is_gt)
            v.tensor_scalar(qb[:, 0:width], src_c, invg[:, col:col + 1], -0.5,
                            ALU.mult, ALU.is_lt)
            ins = v.tensor_sub(qq[:, 0:width], qa[:, 0:width], qb[:, 0:width])
            dv.inc(ins, 1)
        mark = dv.v

        @blk.scalar
        def _(q):
            q.wait_ge(dv.h, mark)
            ins = q.dma_start(out=dst, in_=qq[0:rows, 0:width])
            ss.inc(ins, 16)

    for c in range(2):
        qchunk(w1sb[:, c, :], 0, 128, HID0,
               io["w1q_s"][c * 128:(c + 1) * 128, :])
    for c in range(2):
        qchunk(w2sb[:, c, :], 1, 128, HID0,
               io["w2q_s"][c * 128:(c + 1) * 128, :])
    for c in range(6):
        rows = min(128, 704 - c * 128)
        qchunk(w3sb[:, c, :], 2, rows, F,
               io["w3q_s"][c * 128:c * 128 + rows, :])

    es1.close()
    P.fence()
    _emit_decode(P, io)

    if gather:
        @blk.gpsimd
        def _(g):
            g.wait_ge(ss.h, ss.v)
            for a, b in (("w1q_s", "W1Q"), ("w2q_s", "W2Q"), ("w3q_s", "W3Q"),
                         ("wgq_s", "WGQ"), ("wuq_s", "WUQ"), ("wdq_s", "WDQ")):
                ins = g.collective_compute(
                    "AllGather", ALU.bypass, replica_groups=[list(range(NC))],
                    ins=[io[a].opt()], outs=[io[b].opt()])
                ccs.inc(ins, 1)
                g.wait_ge(ccs.h, ccs.v)


def _emit_decode(P, io):
    nc, blk = P.nc, P.block
    sl, ss, dv = P.sems["sl"], P.sems["ss"], P.sems["dv"]

    es2 = contextlib.ExitStack()
    u8t = P.sbuf("d_u8", [128, MLP1], I16, es2)
    sc = P.sbuf("d_sc", [128, 4096], F32, es2)
    t_lo = P.sbuf("d_lo", [128, 4096], I16, es2)
    t_hi = P.sbuf("d_hi", [128, 4096], I16, es2)
    t1 = P.sbuf("d_t1", [128, 4096], I16, es2)
    t2 = P.sbuf("d_t2", [128, 4096], I16, es2)
    t3 = P.sbuf("d_t3", [128, 4096], I16, es2)
    t4 = P.sbuf("d_t4", [128, 4096], I16, es2)
    vb = P.sbuf("d_vb", [128, 4096], BF16, es2)
    ob = P.sbuf("d_ob", [128, 4096], BF16, es2)

    def decode_plane(v, pl):
        v.tensor_scalar(t1[:], pl, 6, 6, ALU.bitwise_and, ALU.logical_shift_left)
        v.tensor_scalar(t2[:], pl, 8, 12, ALU.bitwise_and, ALU.logical_shift_left)
        v.tensor_scalar(t3[:], t1[:], 1, None, ALU.min)
        v.scalar_tensor_tensor(t1[:], t1[:], float(126 << 7), t2[:],
                               ALU.add, ALU.add)
        v.tensor_scalar(t2[:], pl, 7, None, ALU.bitwise_and)
        v.tensor_scalar(t2[:], t2[:], 1, None, ALU.min)
        v.tensor_scalar(t4[:], pl, 1, None, ALU.bitwise_and)
        v.scalar_tensor_tensor(t3[:], t3[:], 64.0, t4[:],
                               ALU.mult, ALU.mult)
        v.scalar_tensor_tensor(t2[:], t2[:], float(0x3F80), t3[:],
                               ALU.mult, ALU.add)
        v.tensor_tensor(vb[:], t1[:].bitcast(BF16), t2[:].bitcast(BF16),
                        ALU.mult)
        return v.tensor_tensor(ob[:], vb[:], sc[:], ALU.mult)

    def one_matrix(u8_dram, out_dram, npieces, width, scsrc):
        @blk.gpsimd
        def _(q):
            q.wait_ge(dv.h, dv.v)
            ins = q.dma_start(out=u8t[:, 0:npieces * width], in_=u8_dram)
            sl.inc(ins, 16)
        for p in range(npieces):
            @blk.sync
            def _(q):
                q.wait_ge(dv.h, dv.v)
                rows = scsrc(p)
                for r, a in enumerate(rows):
                    ins = q.dma_start(out=sc[16 * r:16 * (r + 1), 0:width],
                                      in_=a.broadcast_to([16, width]))
                    sl.inc(ins, 16)

            @blk.vector
            def _(v):
                v.wait_ge(sl.h, sl.v)
                v.wait_ge(ss.h, ss.v)
                v.tensor_scalar(t_lo[:, 0:width], u8t[:, p * width:(p + 1) * width],
                                15, None, ALU.bitwise_and)
                v.tensor_scalar(t_hi[:, 0:width], u8t[:, p * width:(p + 1) * width],
                                4, None, ALU.logical_shift_right)
                ins = decode_plane(v, t_lo[:, 0:width])
                dv.inc(ins, 1)
                P._dm = dv.v

            @blk.scalar
            def _(q):
                q.wait_ge(dv.h, P._dm)
                ins = q.dma_start(out=out_dram[0][:, p * width:(p + 1) * width],
                                  in_=ob[:, 0:width])
                ss.inc(ins, 16)

            @blk.vector
            def _(v):
                v.wait_ge(ss.h, ss.v)
                ins = decode_plane(v, t_hi[:, 0:width])
                dv.inc(ins, 1)
                P._dm = dv.v

            @blk.scalar
            def _(q):
                q.wait_ge(dv.h, P._dm)
                ins = q.dma_start(out=out_dram[1][:, p * width:(p + 1) * width],
                                  in_=ob[:, 0:width])
                ss.inc(ins, 16)

    for (wname, sname, oname) in (("wgT_s", "gsT_s", "wgq_s"),
                                  ("wuT_s", "usT_s", "wuq_s")):
        gdram = io[sname]

        def scsrc(p, gdram=gdram):
            return [gdram[r:r + 1, p * 4096:(p + 1) * 4096] for r in range(8)]
        one_matrix(io[wname], (io[oname][0:128, :], io[oname][128:256, :]),
                   4, 4096, scsrc)

    ddram = io["wdT_s"].rearrange("(g p) h -> p g h", p=128)
    dsd = io["dsT_s"]

    def dscsrc(p):
        return [(j, dsd[8 * g + r:8 * g + r + 1, :])
                for j, g in enumerate((2 * p, 2 * p + 1)) for r in range(8)]

    _one_matrix_down(P, blk, ddram, dscsrc, io["wdq_s"], u8t, sc, t_lo, t_hi,
                     decode_plane)
    es2.close()
    P.fence()


def _one_matrix_down(P, blk, ddram, dscsrc, out_dram, u8t, sc, t_lo, t_hi,
                     decode_plane):
    sl, ss, dv = P.sems["sl"], P.sems["ss"], P.sems["dv"]

    @blk.gpsimd
    def _(q):
        q.wait_ge(dv.h, dv.v)
        ins = q.dma_start(out=u8t[:].rearrange("p (g h) -> p g h", g=8),
                          in_=ddram)
        sl.inc(ins, 16)
    width = 4096
    for p in range(4):
        @blk.sync
        def _(q):
            q.wait_ge(dv.h, dv.v)
            items = dscsrc(p)
            for i, (j, a) in enumerate(items):
                r = i % 8
                ins = q.dma_start(
                    out=sc[16 * r:16 * (r + 1), j * 2048:(j + 1) * 2048],
                    in_=a.broadcast_to([16, 2048]))
                sl.inc(ins, 16)

        @blk.vector
        def _(v):
            v.wait_ge(sl.h, sl.v)
            v.wait_ge(ss.h, ss.v)
            v.tensor_scalar(t_lo[:, 0:width], u8t[:, p * width:(p + 1) * width],
                            15, None, ALU.bitwise_and)
            v.tensor_scalar(t_hi[:, 0:width], u8t[:, p * width:(p + 1) * width],
                            4, None, ALU.logical_shift_right)
            ins = decode_plane(v, t_lo[:, 0:width])
            dv.inc(ins, 1)
            P._dm = dv.v

        @blk.scalar
        def _(q):
            q.wait_ge(dv.h, P._dm)
            ob = P._sb["d_ob"]
            for j, g in enumerate((2 * p, 2 * p + 1)):
                ins = q.dma_start(
                    out=out_dram[g * 128:(g + 1) * 128, :],
                    in_=ob[:, j * 2048:(j + 1) * 2048])
                ss.inc(ins, 16)

        @blk.vector
        def _(v):
            v.wait_ge(ss.h, ss.v)
            ins = decode_plane(v, t_hi[:, 0:width])
            dv.inc(ins, 1)
            P._dm = dv.v

        @blk.scalar
        def _(q):
            q.wait_ge(dv.h, P._dm)
            ob = P._sb["d_ob"]
            for j, g in enumerate((2 * p, 2 * p + 1)):
                ins = q.dma_start(
                    out=out_dram[1024 + g * 128:1024 + (g + 1) * 128, :],
                    in_=ob[:, j * 2048:(j + 1) * 2048])
                ss.inc(ins, 16)



class LNKit:

    def __init__(self, P, es, W=512):
        self.W = W
        self.sq = [P.sbuf(f"ln_sq{i}", [128, 4, W], F32, es) for i in range(2)]
        self.tb = P.sbuf("ln_tb", [128, 16, W], F32, es)
        self.strow = P.sbuf("ln_st", [1, 6, W], F32, es)
        self.Rb = P.sbuf("ln_Rb", [128, W], F32, es)
        self.Mb = P.sbuf("ln_Mb", [128, W], F32, es)
        self.ps_sum = P.psum("ln_pssum", [1, W], F32, es)
        self.ps_sq = P.psum("ln_pssq", [1, W], F32, es)
        self.ps_R = P.psum("ln_psR", [128, W], F32, es)
        self.ps_M = P.psum("ln_psM", [128, W], F32, es)


def emit_ln(P, kit, src, dst, W, eps, gcol=None, bcol=None, inplace=False,
            srcb=None, dstb=None):
    blk = P.block
    pe, dv, ac = P.sems["pe"], P.sems["dv"], P.sems["ac"]
    ones_c, ones_r = P.ones_c, P.ones_r
    sq, strow, Rb, Mb = kit.sq, kit.strow, kit.Rb, kit.Mb

    ac0, pe0 = ac.v, pe.v
    sl, sw, dv0 = P.sems["sl"], P.sems["sw"], P.sems["dv"]
    for g in range(4):
        @blk.scalar
        def _(q, g=g):
            if g == 0:
                q.wait_ge(sl.h, sl.v)
                q.wait_ge(dv0.h, dv0.v)
            if g >= 2:
                q.wait_ge(pe.h, pe0 + 4 * (g - 1))
            for j in range(4):
                ins = q.activation(sq[g % 2][:, j, 0:W], src(4 * g + j),
                                   AF.Square)
            ac.inc(ins, 1)

        @blk.tensor
        def _(t, g=g):
            t.wait_ge(ac.h, ac0 + g + 1)
            for j in range(4):
                c = 4 * g + j
                t.matmul(kit.ps_sum[0:1, 0:W], ones_c[:], src(c),
                         start=(c == 0), stop=(c == FCH - 1))
                ins = t.matmul(kit.ps_sq[0:1, 0:W], ones_c[:],
                               sq[g % 2][:, j, 0:W],
                               start=(c == 0), stop=(c == FCH - 1))
                pe.inc(ins, 1)

    @blk.vector
    def _(v):
        v.wait_ge(pe.h, pe.v)

        def step(ins):
            dv.inc(ins, 1)
            v.wait_ge(dv.h, dv.v)
        step(v.tensor_scalar(strow[:, 0, 0:W], kit.ps_sum[0:1, 0:W],
                             1.0 / F, None, ALU.mult))
        step(v.tensor_scalar(strow[:, 1, 0:W], kit.ps_sq[0:1, 0:W],
                             1.0 / F, None, ALU.mult))
        step(v.tensor_tensor(strow[:, 4, 0:W], strow[:, 0, 0:W],
                             strow[:, 0, 0:W], ALU.mult))
        step(v.tensor_sub(strow[:, 1, 0:W], strow[:, 1, 0:W],
                          strow[:, 4, 0:W]))

    epsb = P.constcol(eps)
    mhalf = P.constcol(-0.5)

    @blk.scalar
    def _(q):
        q.wait_ge(dv.h, dv.v)
        ins = q.activation(strow[:, 4, 0:W], strow[:, 1, 0:W], AF.Ln,
                           bias=epsb[0:1, 0:1])
        ac.inc(ins, 1)
        q.wait_ge(ac.h, ac.v)
        ins = q.activation(strow[:, 2, 0:W], strow[:, 4, 0:W], AF.Exp,
                           scale=mhalf[0:1, 0:1])
        ac.inc(ins, 1)

    @blk.vector
    def _(v):
        v.wait_ge(ac.h, ac.v)
        ins = v.tensor_mul(strow[:, 3, 0:W], strow[:, 0, 0:W], strow[:, 2, 0:W])
        dv.inc(ins, 1)

    @blk.tensor
    def _(t):
        t.wait_ge(dv.h, dv.v)
        t.matmul(kit.ps_R[:, 0:W], ones_r[0:1, :], strow[:, 2, 0:W],
                 start=True, stop=True)
        ins = t.matmul(kit.ps_M[:, 0:W], ones_r[0:1, :], strow[:, 3, 0:W],
                       start=True, stop=True)
        pe.inc(ins, 1)

    @blk.vector
    def _(v):
        v.wait_ge(pe.h, pe.v)
        v.tensor_copy(Rb[:, 0:W], kit.ps_R[:, 0:W])
        ins = v.tensor_copy(Mb[:, 0:W], kit.ps_M[:, 0:W])
        dv.inc(ins, 1)
        v.wait_ge(dv.h, dv.v)
        flush = W <= 128

        def stepf(ins):
            dv.inc(ins, 1)
            if flush:
                v.wait_ge(dv.h, dv.v)
        if W >= 256 and srcb is not None:
            Rb3 = Rb[:, 0:W].unsqueeze(1).broadcast_to([128, 16, W])
            Mb3 = Mb[:, 0:W].unsqueeze(1).broadcast_to([128, 16, W])
            stepf(v.tensor_tensor(kit.tb[:, :, 0:W], srcb, Rb3, ALU.mult))
            stepf(v.tensor_tensor(kit.tb[:, :, 0:W], kit.tb[:, :, 0:W], Mb3,
                                  ALU.subtract))
            if gcol is None:
                if dstb is not None:
                    stepf(v.tensor_copy(dstb, kit.tb[:, :, 0:W]))
                else:
                    for c in range(FCH):
                        stepf(v.tensor_copy(dst(c), kit.tb[:, c, 0:W]))
            else:
                for c in range(FCH):
                    stepf(v.tensor_scalar(dst(c), kit.tb[:, c, 0:W],
                                          gcol(c), bcol(c),
                                          ALU.mult, ALU.add))
        else:
            for c in range(FCH):
                tmp = sq[c % 2][:, 0, 0:W]
                stepf(v.tensor_mul(tmp, src(c), Rb[:, 0:W]))
                if gcol is None:
                    stepf(v.tensor_sub(dst(c), tmp, Mb[:, 0:W]))
                else:
                    stepf(v.tensor_sub(tmp, tmp, Mb[:, 0:W]))
                    stepf(v.tensor_scalar(dst(c), tmp, gcol(c), bcol(c),
                                          ALU.mult, ALU.add))


def emit_b01(P, io):
    nc, blk = P.nc, P.block
    sl, sw, ss = P.sems["sl"], P.sems["sw"], P.sems["ss"]
    pe, dv, ac = P.sems["pe"], P.sems["dv"], P.sems["ac"]

    es = contextlib.ExitStack()
    vecs = {}
    for nm in ("ln1_g", "ln1_b", "tm_g", "tm_b", "tf", "td", "bo", "maskm"):
        vecs[nm] = P.sbuf("v_" + nm, [128, 16] if nm != "maskm" else [128, 1],
                          F32, es)
    tf1m = P.sbuf("v_tf1m", [128, 16], F32, es)
    vmix = P.sbuf("b_vmix", [128, 16], F32, es)
    vmixb = P.sbuf("b_vmixb", [128, 16], BF16, es)
    wovec = P.sbuf("b_wo", [128, 16], F32, es)
    xlp = P.sbuf("b_xlp", [128, 16], F32, es)

    @blk.vector
    def _(v):
        v.memset(xlp[:], 0.0)

    @blk.sync
    def _(q):
        for nm, t in vecs.items():
            src = io[nm]
            ins = q.dma_start(out=t[:], in_=src)
            sl.inc(ins, 16)

    @blk.vector
    def _(v):
        v.wait_ge(sl.h, sl.v)
        ins = v.tensor_scalar(tf1m[:], vecs["tf"][:], -1.0, 1.0,
                              ALU.mult, ALU.add)
        dv.inc(ins, 1)

    kit = LNKit(P, es)

    es0 = contextlib.ExitStack()
    xt = P.sbuf("b0_xt", [128, 16, 512], F32, es0)
    wtile = P.sbuf("b0_wt", [128, 16, 2048], BF16, es0)
    xlnb = P.sbuf("b0_xlnb", [128, 16, TAIL], BF16, es0)
    tpb = P.sbuf("b0_tpb", [128, TAIL], F32, es0)
    dec = P.sbuf("b0_dec", [128, TAIL], F32, es0)
    vsb = P.sbuf("b0_vsb", [128, TAIL], F32, es0)
    ps_v = P.psum("b0_psv", [128, 1024], F32, es0)
    ps_w = P.psum("b0_psw", [128, 16], F32, es0)

    @blk.sync
    def _(q):
        ins = q.dma_start(out=tpb[:], in_=io["tpos"].broadcast_to([128, TAIL]))
        sl.inc(ins, 16)

    @blk.gpsimd
    def _(g):
        for k in range(FCH):
            ins = g.dma_start(out=wtile[:, k, :],
                              in_=io["Wv"][128 * k:128 * (k + 1), :])
            sw.inc(ins, 16)

    @blk.vector
    def _(v):
        v.wait_ge(sl.h, sl.v)

    for half in range(2):
        s = 512 * half

        @blk.sync
        def _(q, s=s):
            q.wait_ge(dv.h, dv.v)
            ins = q.dma_start(
                out=xt[:],
                in_=io["xtail"][:, s:s + 512].rearrange("(c p) t -> p c t",
                                                        p=128))
            sl.inc(ins, 16)
        emit_ln(P, kit, lambda c: xt[:, c, :],
                lambda c: xt[:, c, :], 512, EPS0,
                srcb=xt[:], dstb=xt[:])
        emit_ln(P, kit, lambda c: xt[:, c, :],
                lambda c: xt[:, c, :], 512, EPSL,
                gcol=lambda c: vecs["ln1_g"][:, c:c + 1],
                bcol=lambda c: vecs["ln1_b"][:, c:c + 1],
                srcb=xt[:])
        emit_ln(P, kit, lambda c: xt[:, c, :],
                lambda c, s=s: xlnb[:, c, s:s + 512], 512, EPSL,
                gcol=lambda c: vecs["tm_g"][:, c:c + 1],
                bcol=lambda c: vecs["tm_b"][:, c:c + 1],
                srcb=xt[:])

    for hh in range(FCH):
        @blk.tensor
        def _(t, hh=hh):
            t.wait_ge(sw.h, sw.v)
            t.wait_ge(dv.h, dv.v)
            for tt in range(2):
                for k in range(FCH):
                    ins = t.matmul(ps_v[:, 512 * tt:512 * (tt + 1)],
                                   wtile[:, k, 128 * hh:128 * (hh + 1)],
                                   xlnb[:, k, 512 * tt:512 * (tt + 1)],
                                   start=(k == 0), stop=(k == FCH - 1))
                pe.inc(ins, 1)

        @blk.scalar
        def _(q, hh=hh):
            q.wait_ge(sl.h, sl.v)
            q.wait_ge(dv.h, dv.v)
            ins = q.activation(dec[:], tpb[:], AF.Exp,
                               scale=vecs["td"][:, hh:hh + 1])
            ac.inc(ins, 1)

        @blk.vector
        def _(v, hh=hh):
            v.wait_ge(pe.h, pe.v)
            v.tensor_copy(vsb[:, 0:512], ps_v[:, 0:512])
            v.tensor_copy(vsb[:, 512:1024], ps_v[:, 512:1024])
            v.wait_ge(ac.h, ac.v)
            v.tensor_mul(vsb[:], vsb[:], dec[:])
            ins = v.tensor_reduce(vmix[:, hh:hh + 1], vsb[:], AX.X, ALU.add)
            dv.inc(ins, 1)

    @blk.vector
    def _(v):
        v.wait_ge(dv.h, dv.v)
        ins = v.tensor_copy(vmixb[:], vmix[:])
        dv.inc(ins, 1)

    @blk.gpsimd
    def _(g):
        g.wait_ge(pe.h, pe.v)
        for k in range(FCH):
            ins = g.dma_start(out=wtile[:, k, :],
                              in_=io["Wo"][128 * k:128 * (k + 1), :])
            sw.inc(ins, 16)

    @blk.tensor
    def _(t):
        t.wait_ge(sw.h, sw.v)
        t.wait_ge(dv.h, dv.v)
        for m in range(FCH):
            for k in range(FCH):
                ins = t.matmul(ps_w[:, m:m + 1],
                               wtile[:, k, 128 * m:128 * (m + 1)],
                               vmixb[:, k:k + 1],
                               start=(k == 0), stop=(k == FCH - 1))
        pe.inc(ins, 1)

    @blk.vector
    def _(v):
        v.wait_ge(pe.h, pe.v)
        ins = v.tensor_add(wovec[:], ps_w[:], vecs["bo"][:])
        dv.inc(ins, 1)

    es0.close()
    P.fence()

    es1 = contextlib.ExitStack()
    x0t = P.sbuf("b1_x0", [128, 16, 256], F32, es1)
    yt = P.sbuf("b1_y", [128, 16, 256], F32, es1)
    ct = P.sbuf("b1_c", [128, 16, 256], F32, es1)
    mixb = P.sbuf("b1_mix", [128, 16, 256], BF16, es1)
    wr = P.sbuf("b1_wr", [128, 16, 2048], BF16, es1)
    rb = [P.sbuf(f"b1_r{i}", [128, 512], F32, es1) for i in range(2)]
    x2c = [P.sbuf(f"b1_x2c{i}", [128, 512], F32, es1) for i in range(2)]
    s1c = [P.sbuf(f"b1_s1c{i}", [128, 512], F32, es1) for i in range(2)]
    ps_r = P.psum("b1_psr", [128, 1024], F32, es1)

    @blk.gpsimd
    def _(g):
        for k in range(FCH):
            ins = g.dma_start(out=wr[:, k, :],
                              in_=io["Wr"][128 * k:128 * (k + 1), :])
            sw.inc(ins, 16)

    tiles = [(0, 2, True)] + [(2 + 256 * i, 256, False) for i in range(8)]
    for base, W, halo in tiles:
        @blk.sync
        def _(q, base=base, W=W):
            q.wait_ge(pe.h, pe.v)
            q.wait_ge(dv.h, dv.v)
            q.wait_ge(ac.h, ac.v)
            ins = q.dma_start(
                out=x0t[:, :, 0:W],
                in_=io["x0"][:, base:base + W].rearrange("(c p) t -> p c t",
                                                         p=128))
            sl.inc(ins, 16)

        @blk.vector
        def _(v):
            v.wait_ge(sl.h, sl.v)

        emit_ln(P, kit, lambda c, W=W: x0t[:, c, 0:W],
                lambda c, W=W: yt[:, c, 0:W], W, EPS0,
                srcb=x0t[:, :, 0:W], dstb=yt[:, :, 0:W])
        emit_ln(P, kit, lambda c, W=W: yt[:, c, 0:W],
                lambda c, W=W: ct[:, c, 0:W], W, EPSL,
                gcol=lambda c: vecs["ln1_g"][:, c:c + 1],
                bcol=lambda c: vecs["ln1_b"][:, c:c + 1],
                srcb=yt[:, :, 0:W])
        emit_ln(P, kit, lambda c, W=W: ct[:, c, 0:W],
                lambda c, W=W: ct[:, c, 0:W], W, EPSL,
                gcol=lambda c: vecs["tm_g"][:, c:c + 1],
                bcol=lambda c: vecs["tm_b"][:, c:c + 1], inplace=True,
                srcb=ct[:, :, 0:W])

        @blk.vector
        def _(v, W=W):
            for c in range(FCH):
                tmp = kit.sq[0][:, 0, 0:W]
                ins = v.tensor_scalar(tmp, ct[:, c, 0:W], vecs["tf"][:, c:c + 1],
                                      None, ALU.mult)
                dv.inc(ins, 1)
                v.wait_ge(dv.h, dv.v)
                if W > 1:
                    v.scalar_tensor_tensor(
                        mixb[:, c, 1:W], ct[:, c, 0:W - 1],
                        tf1m[:, c:c + 1], tmp[:, 1:W], ALU.mult, ALU.add)
                v.scalar_tensor_tensor(
                    mixb[:, c, 0:1], xlp[:, c:c + 1], tf1m[:, c:c + 1],
                    tmp[:, 0:1], ALU.mult, ALU.add)
            ins = v.tensor_copy(xlp[:], ct[:, :, W - 1])
            dv.inc(ins, 1)
            if halo:
                v.wait_ge(dv.h, dv.v)
                ins = v.tensor_scalar(xlp[:], xlp[:], vecs["maskm"][:, 0:1],
                                      None, ALU.mult)
                dv.inc(ins, 1)

        mix_mark = dv.v
        rmk = {"sig": [0, 0], "dve": [0, 0], "st": [0, 0]}
        if halo and "DBGxlp" in io:
            @blk.scalar
            def _(q):
                q.wait_ge(dv.h, dv.v)
                ins = q.dma_start(out=io["DBGxlp"], in_=xlp[:])
                ss.inc(ins, 16)
                ins = q.dma_start(out=io["DBGmix"],
                                  in_=mixb[:, :, 0:2])
                ss.inc(ins, 16)
        for m in range(FCH):
            par = m % 2

            @blk.tensor
            def _(t, m=m, W=W, par=par):
                t.wait_ge(dv.h, mix_mark)
                t.wait_ge(ac.h, rmk["sig"][par])
                for k in range(FCH):
                    ins = t.matmul(ps_r[:, 512 * par:512 * par + W],
                                   wr[:, k, 128 * m:128 * (m + 1)],
                                   mixb[:, k, 0:W],
                                   start=(k == 0), stop=(k == FCH - 1))
                pe.inc(ins, 1)
            pmr = pe.v

            @blk.scalar
            def _(q, m=m, W=W, par=par, pmr=pmr):
                q.wait_ge(pe.h, pmr)
                q.wait_ge(dv.h, rmk["dve"][par])
                ins = q.activation(rb[par][:, 0:W],
                                   ps_r[:, 512 * par:512 * par + W], AF.Sigmoid)
                ac.inc(ins, 1)
            rmk["sig"][par] = ac.v
            amr = ac.v

            @blk.vector
            def _(v, m=m, W=W, par=par, amr=amr):
                v.wait_ge(ac.h, amr)
                v.wait_ge(ss.h, rmk["st"][par])
                v.scalar_tensor_tensor(x2c[par][:, 0:W], rb[par][:, 0:W],
                                       wovec[:, m:m + 1], yt[:, m, 0:W],
                                       ALU.mult, ALU.add)
                ins = v.tensor_add(s1c[par][:, 0:W], x2c[par][:, 0:W],
                                   x0t[:, m, 0:W])
                dv.inc(ins, 1)
            rmk["dve"][par] = dv.v
            dmr = dv.v

            @blk.scalar
            def _(q, m=m, W=W, halo=halo, base=base, par=par, dmr=dmr):
                q.wait_ge(dv.h, dmr)
                if halo:
                    ins = q.dma_start(out=io["X2"][128 * m:128 * (m + 1), 0:1],
                                      in_=x2c[par][:, 1:2],
                                      allow_slow_non_contiguous=True)
                    ss.inc(ins, 16)
                else:
                    ins = q.dma_start(
                        out=io["X2"][128 * m:128 * (m + 1), base - 1:base - 1 + W],
                        in_=x2c[par][:, 0:W])
                    ss.inc(ins, 16)
                    ins = q.dma_start(
                        out=io["S1P"][128 * m:128 * (m + 1),
                                      base - 2:base - 2 + W],
                        in_=s1c[par][:, 0:W])
                    ss.inc(ins, 16)
                rmk["st"][par] = ss.v

    es1.close()
    es.close()
    P.fence()



def emit_proj(P, es, rhs_chunk, n_total, w_dram, w_is_f32, epilogue, out_dram,
              tag):
    blk = P.block
    sl, sw, ss = P.sems["sl"], P.sems["sw"], P.sems["ss"]
    pe, dv, ac = P.sems["pe"], P.sems["dv"], P.sems["ac"]
    npieces = n_total // 512

    wbuf = [P.sbuf(f"{tag}_w{i}", [128, 16, 512], BF16, es) for i in range(2)]
    ob = [P.sbuf(f"{tag}_ob{i}", [128, 4, T], BF16, es) for i in range(2)]
    tmp = [P.sbuf(f"{tag}_tmp{i}", [128, 512], F32, es) for i in range(2)]
    ps = P.psum(f"{tag}_ps", [128, 2048], F32, es)
    wq_marks = [0, 0]
    store_mark = [0, 0]
    bank_ep = [(None, 0)] * 4
    tmp_ep = [0, 0]

    for p in range(npieces):
        wb = wbuf[p % 2]
        obb = ob[p % 2]
        wqueue = blk.gpsimd if w_is_f32 else blk.sync
        wsem = sw if w_is_f32 else sl

        @wqueue
        def _(g, p=p, wb=wb):
            if p >= 2:
                g.wait_ge(pe.h, wq_marks[p % 2])
            for k in range(FCH):
                ins = g.dma_start(
                    out=wb[:, k, :],
                    in_=w_dram[128 * k:128 * (k + 1), 512 * p:512 * (p + 1)])
                wsem.inc(ins, 16)
        wload = wsem.v

        for ncs in range(4):
            for ts in range(4):
                bank = (ncs * 4 + ts) % 4

                @blk.tensor
                def _(t, wb=wb, ncs=ncs, ts=ts, bank=bank, wload=wload):
                    t.wait_ge((sw if w_is_f32 else sl).h, wload)
                    bs, bm = bank_ep[bank]
                    if bs is not None:
                        t.wait_ge(bs.h, bm)
                    for k in range(FCH):
                        ins = t.matmul(
                            ps[:, 512 * bank:512 * (bank + 1)],
                            wb[:, k, 128 * ncs:128 * (ncs + 1)],
                            rhs_chunk(k)[:, 512 * ts:512 * (ts + 1)],
                            start=(k == 0), stop=(k == FCH - 1))
                    pe.inc(ins, 1)
                pm = pe.v
                psv = ps[:, 512 * bank:512 * (bank + 1)]
                oslice = obb[:, ncs, 512 * ts:512 * (ts + 1)]

                if epilogue == "relu2":
                    tb = (ncs * 4 + ts) % 2

                    @blk.vector
                    def _(v, pm=pm, psv=psv, tb=tb):
                        v.wait_ge(pe.h, pm)
                        v.wait_ge(ac.h, tmp_ep[tb])
                        ins = v.tensor_scalar(tmp[tb][:], psv, 0.0, None,
                                              ALU.max)
                        dv.inc(ins, 1)
                    bank_ep[bank] = (dv, dv.v)
                    dm = dv.v

                    @blk.scalar
                    def _(q, dm=dm, oslice=oslice, tb=tb, p=p):
                        q.wait_ge(dv.h, dm)
                        q.wait_ge(ss.h, store_mark[p % 2])
                        ins = q.activation(oslice, tmp[tb][:], AF.Square)
                        ac.inc(ins, 1)
                    tmp_ep[tb] = ac.v
                else:
                    fn = AF.Sigmoid if epilogue == "sigmoid" else AF.Identity

                    @blk.scalar
                    def _(q, pm=pm, psv=psv, oslice=oslice, fn=fn, p=p):
                        q.wait_ge(pe.h, pm)
                        q.wait_ge(ss.h, store_mark[p % 2])
                        ins = q.activation(oslice, psv, fn)
                        ac.inc(ins, 1)
                    bank_ep[bank] = (ac, ac.v)
        wq_marks[p % 2] = pe.v

        @blk.sync
        def _(q, p=p, obb=obb):
            q.wait_ge(ac.h, ac.v)
            ins = q.dma_start(
                out=out_dram[512 * p:512 * (p + 1), :].rearrange(
                    "(c p2) t -> p2 c t", p2=128),
                in_=obb[:])
            ss.inc(ins, 16)
        store_mark[p % 2] = ss.v


def emit_down(P, es, a_dram, n_chunks, w_dram, w_is_f32, m_total, final, tag):
    blk = P.block
    sl, sw, ss = P.sems["sl"], P.sems["sw"], P.sems["ss"]
    pe, dv, ac = P.sems["pe"], P.sems["dv"], P.sems["ac"]
    MH = m_total // 2
    mh_ch = MH // 128

    wg = [P.sbuf(f"{tag}_wg{i}", [128, 8, MH], BF16, es) for i in range(2)]
    at = [P.sbuf(f"{tag}_at{i}", [128, 8, 512], BF16, es) for i in range(2)]
    acc = P.sbuf(f"{tag}_acc", [128, mh_ch, T], F32, es)
    ps = P.psum(f"{tag}_ps", [128, 2048], F32, es)

    groups = []
    n0 = 0
    while n0 < n_chunks:
        gn = min(8, n_chunks - n0)
        groups.append((n0, gn))
        n0 += gn
    wmarks = [0, 0]
    amarks = [0, 0]
    bank_ev = [0, 0, 0, 0]

    for half in range(2):
        for gi, (g0, gn) in enumerate(groups):
            gidx = half * len(groups) + gi
            wb = wg[gidx % 2]
            wqueue = blk.gpsimd if w_is_f32 else blk.sync
            wsem = sw if w_is_f32 else sl

            @wqueue
            def _(g, g0=g0, gn=gn, wb=wb, gidx=gidx, half=half):
                if gidx >= 2:
                    g.wait_ge(pe.h, wmarks[gidx % 2])
                for j in range(gn):
                    ins = g.dma_start(
                        out=wb[:, j, :],
                        in_=w_dram[128 * (g0 + j):128 * (g0 + j + 1),
                                   MH * half:MH * (half + 1)])
                    wsem.inc(ins, 16)

            for ts in range(4):
                tsi = gidx * 4 + ts
                ab = at[tsi % 2]

                @blk.sync
                def _(q, g0=g0, gn=gn, ab=ab, tsi=tsi, ts=ts):
                    if tsi >= 2:
                        q.wait_ge(pe.h, amarks[tsi % 2])
                    ins = q.dma_start(
                        out=ab[:, 0:gn, :],
                        in_=a_dram[128 * g0:128 * (g0 + gn),
                                   512 * ts:512 * (ts + 1)].rearrange(
                                       "(c p2) t -> p2 c t", p2=128))
                    sl.inc(ins, 16)

                for mc in range(mh_ch):
                    bank = (tsi * mh_ch + mc) % 4

                    @blk.tensor
                    def _(t, gn=gn, wb=wb, ab=ab, mc=mc, bank=bank,
                          wl=None, al=None):
                        t.wait_ge(sw.h, sw.v)
                        t.wait_ge(sl.h, sl.v)
                        bm = bank_ev[bank]
                        if bm:
                            t.wait_ge(dv.h, bm)
                        for j in range(gn):
                            ins = t.matmul(
                                ps[:, 512 * bank:512 * (bank + 1)],
                                wb[:, j, 128 * mc:128 * (mc + 1)],
                                ab[:, j, :],
                                start=(j == 0), stop=(j == gn - 1))
                        pe.inc(ins, 1)
                    pm = pe.v

                    @blk.vector
                    def _(v, gi=gi, mc=mc, ts=ts, bank=bank, pm=pm):
                        v.wait_ge(pe.h, pm)
                        dst = acc[:, mc, 512 * ts:512 * (ts + 1)]
                        srcp = ps[:, 512 * bank:512 * (bank + 1)]
                        if gi == 0:
                            ins = v.tensor_copy(dst, srcp)
                        else:
                            ins = v.tensor_add(dst, dst, srcp)
                        dv.inc(ins, 1)
                    bank_ev[bank] = dv.v
                amarks[tsi % 2] = pe.v
            wmarks[gidx % 2] = pe.v

        for mc in range(mh_ch):
            final(half * mh_ch + mc, 0, acc[:, mc, :])
        P.fence()



def emit_b2(P, io):
    nc, blk = P.nc, P.block
    sl, sw, ss = P.sems["sl"], P.sems["sw"], P.sems["ss"]
    pe, dv, ac = P.sems["pe"], P.sems["dv"], P.sems["ac"]

    es = contextlib.ExitStack()
    vecs = {}
    for nm in ("ln2_g", "ln2_b", "cm_g", "cm_b", "maskm"):
        vecs[nm] = P.sbuf("v_" + nm, [128, 16] if nm != "maskm" else [128, 1],
                          F32, es)
    xc = P.sbuf("b2_xc", [128, 16, T + 1], BF16, es)

    @blk.sync
    def _(q):
        for nm, t in vecs.items():
            ins = q.dma_start(out=t[:], in_=io[nm])
            sl.inc(ins, 16)

    es1 = contextlib.ExitStack()
    kit = LNKit(P, es1)
    x2t = P.sbuf("b2_x2t", [128, 16, 512], F32, es1)
    ut = P.sbuf("b2_ut", [128, 16, 512], F32, es1)

    tiles = [(0, 1)] + [(1 + 512 * i, 512) for i in range(4)]
    for base, W in tiles:
        @blk.sync
        def _(q, base=base, W=W):
            q.wait_ge(dv.h, dv.v)
            q.wait_ge(ac.h, ac.v)
            ins = q.dma_start(
                out=x2t[:, :, 0:W],
                in_=io["X2"][:, base:base + W].rearrange("(c p) t -> p c t",
                                                         p=128),
                allow_slow_non_contiguous=(W < 8))
            sl.inc(ins, 16)
        emit_ln(P, kit, lambda c, W=W: x2t[:, c, 0:W],
                lambda c, W=W: ut[:, c, 0:W], W, EPSL,
                gcol=lambda c: vecs["ln2_g"][:, c:c + 1],
                bcol=lambda c: vecs["ln2_b"][:, c:c + 1],
                srcb=x2t[:, :, 0:W])
        emit_ln(P, kit, lambda c, W=W: ut[:, c, 0:W],
                lambda c, W=W, base=base: xc[:, c, base:base + W], W, EPSL,
                gcol=lambda c: vecs["cm_g"][:, c:c + 1],
                bcol=lambda c: vecs["cm_b"][:, c:c + 1],
                srcb=ut[:, :, 0:W])

    @blk.vector
    def _(v):
        v.wait_ge(dv.h, dv.v)
        for c in range(FCH):
            ins = v.tensor_scalar(xc[:, c, 0:1], xc[:, c, 0:1],
                                  vecs["maskm"][:, 0:1], None, ALU.mult)
            dv.inc(ins, 1)
    es1.close()
    P.fence()

    es2 = contextlib.ExitStack()
    emit_proj(P, es2, lambda k: xc[:, k, 1:T + 1], F, io["Wcr"], True,
              "sigmoid", io["RC"], "rc")
    es2.close()
    P.fence()
    es3 = contextlib.ExitStack()
    emit_proj(P, es3, lambda k: xc[:, k, 0:T], I0, io["Wck"], True,
              "relu2", io["KC"], "kc")
    es3.close()
    es.close()
    P.fence()

    es4 = contextlib.ExitStack()
    rct = [P.sbuf(f"b2_rct{i}", [128, T], BF16, es4) for i in range(2)]
    s1t = [P.sbuf(f"b2_s1t{i}", [128, T], F32, es4) for i in range(2)]
    tmp2 = [P.sbuf(f"b2_tmp2{i}", [128, T], F32, es4) for i in range(2)]
    fmarks = [[0, 0], [0, 0]]

    def final(mc, ts, acc_ap):
        if ts != 0:
            return
        b = mc % 2

        @blk.sync
        def _(q, mc=mc, b=b):
            q.wait_ge(dv.h, fmarks[b][0])
            ins = q.dma_start(out=rct[b][:],
                              in_=io["RC"][128 * mc:128 * (mc + 1), :])
            sl.inc(ins, 16)
            ins = q.dma_start(out=s1t[b][:],
                              in_=io["S1P"][128 * mc:128 * (mc + 1), :])
            sl.inc(ins, 16)
        lm = sl.v

        @blk.vector
        def _(v, acc_ap=acc_ap, b=b, lm=lm):
            v.wait_ge(sl.h, lm)
            v.wait_ge(ss.h, fmarks[b][1])
            v.tensor_mul(tmp2[b][:], rct[b][:], acc_ap)
            ins = v.tensor_add(tmp2[b][:], tmp2[b][:], s1t[b][:])
            dv.inc(ins, 1)
        fmarks[b][0] = dv.v

        @blk.scalar
        def _(q, mc=mc, b=b):
            q.wait_ge(dv.h, fmarks[b][0])
            ins = q.dma_start(out=io["S1"][128 * mc:128 * (mc + 1), :],
                              in_=tmp2[b][:])
            ss.inc(ins, 16)
        fmarks[b][1] = ss.v

    emit_down(P, es4, io["KC"], 64, io["Wcv"], True, F, final, "vc")
    es4.close()
    P.fence()



def emit_stream_ln(P, es, src_dram, dst_sbuf, tag):
    blk = P.block
    sl, dv, ac = P.sems["sl"], P.sems["dv"], P.sems["ac"]
    esk = contextlib.ExitStack()
    kit = LNKit(P, esk)
    st = P.sbuf(f"{tag}_st", [128, 16, 512], F32, esk)
    for i in range(4):
        @blk.sync
        def _(q, i=i):
            q.wait_ge(dv.h, dv.v)
            q.wait_ge(ac.h, ac.v)
            ins = q.dma_start(
                out=st[:],
                in_=src_dram[:, 512 * i:512 * (i + 1)].rearrange(
                    "(c p) t -> p c t", p=128))
            sl.inc(ins, 16)
        emit_ln(P, kit, lambda c: st[:, c, :],
                lambda c, i=i: dst_sbuf[:, c, 512 * i:512 * (i + 1)],
                512, EPS0, srcb=st[:],
                dstb=dst_sbuf[:, :, 512 * i:512 * (i + 1)])
    esk.close()
    P.fence()


def emit_proj2(P, es, rhs_chunk, n_total, wA_dram, wB_dram, act_fn, out_dram,
               tag):
    blk = P.block
    sl, sw, ss = P.sems["sl"], P.sems["sw"], P.sems["ss"]
    pe, dv, ac = P.sems["pe"], P.sems["dv"], P.sems["ac"]
    npieces = n_total // 512

    wbufA = [P.sbuf(f"{tag}_wa{i}", [128, 16, 512], BF16, es) for i in range(2)]
    wbufB = [P.sbuf(f"{tag}_wb{i}", [128, 16, 512], BF16, es) for i in range(2)]
    ob = [P.sbuf(f"{tag}_ob{i}", [128, 4, T], BF16, es) for i in range(2)]
    tmp = [P.sbuf(f"{tag}_tmp{i}", [128, 512], F32, es) for i in range(2)]
    ps = P.psum(f"{tag}_ps", [128, 2048], F32, es)
    wq_marks = [0, 0]
    store_mark = [0, 0]
    pair_ep = [(0, 0)] * 2
    tmp_ep = [0, 0]

    for p in range(npieces):
        wa, wb = wbufA[p % 2], wbufB[p % 2]
        obb = ob[p % 2]

        @blk.sync
        def _(q, p=p, wa=wa, wb=wb):
            if p >= 2:
                q.wait_ge(pe.h, wq_marks[p % 2])
            for k in range(FCH):
                ins = q.dma_start(
                    out=wa[:, k, :],
                    in_=wA_dram[128 * k:128 * (k + 1), 512 * p:512 * (p + 1)])
                sl.inc(ins, 16)
                ins = q.dma_start(
                    out=wb[:, k, :],
                    in_=wB_dram[128 * k:128 * (k + 1), 512 * p:512 * (p + 1)])
                sl.inc(ins, 16)
        wload = sl.v

        for ncs in range(4):
            for ts in range(4):
                pair = (ncs * 4 + ts) % 2

                @blk.tensor
                def _(t, wa=wa, wb=wb, ncs=ncs, ts=ts, pair=pair, wload=wload):
                    t.wait_ge(sl.h, wload)
                    am, dm = pair_ep[pair]
                    if am:
                        t.wait_ge(ac.h, am)
                    if dm:
                        t.wait_ge(dv.h, dm)
                    for k in range(FCH):
                        ins = t.matmul(
                            ps[:, 1024 * pair:1024 * pair + 512],
                            wa[:, k, 128 * ncs:128 * (ncs + 1)],
                            rhs_chunk(k)[:, 512 * ts:512 * (ts + 1)],
                            start=(k == 0), stop=(k == FCH - 1))
                    for k in range(FCH):
                        ins2 = t.matmul(
                            ps[:, 1024 * pair + 512:1024 * pair + 1024],
                            wb[:, k, 128 * ncs:128 * (ncs + 1)],
                            rhs_chunk(k)[:, 512 * ts:512 * (ts + 1)],
                            start=(k == 0), stop=(k == FCH - 1))
                    pe.inc(ins2, 1)
                pm = pe.v
                psA = ps[:, 1024 * pair:1024 * pair + 512]
                psB = ps[:, 1024 * pair + 512:1024 * pair + 1024]
                oslice = obb[:, ncs, 512 * ts:512 * (ts + 1)]
                tb = (ncs * 4 + ts) % 2

                @blk.scalar
                def _(q, pm=pm, psA=psA, tb=tb):
                    q.wait_ge(pe.h, pm)
                    q.wait_ge(dv.h, tmp_ep[tb])
                    ins = q.activation(tmp[tb][:], psA, act_fn)
                    ac.inc(ins, 1)
                am = ac.v

                @blk.vector
                def _(v, am=am, psB=psB, oslice=oslice, tb=tb, p=p):
                    v.wait_ge(ac.h, am)
                    v.wait_ge(ss.h, store_mark[p % 2])
                    ins = v.tensor_mul(oslice, tmp[tb][:], psB)
                    dv.inc(ins, 1)
                pair_ep[pair] = (am, dv.v)
                tmp_ep[tb] = dv.v
        wq_marks[p % 2] = pe.v

        @blk.scalar
        def _(q, p=p, obb=obb):
            q.wait_ge(dv.h, dv.v)
            ins = q.dma_start(
                out=out_dram[512 * p:512 * (p + 1), :].rearrange(
                    "(c p2) t -> p2 c t", p2=128),
                in_=obb[:])
            ss.inc(ins, 16)
        store_mark[p % 2] = ss.v


def _final_stream_add(P, sin_dram, sout_dram, es):
    blk = P.block
    sl, ss, dv = P.sems["sl"], P.sems["ss"], P.sems["dv"]
    s_t = [P.sbuf(f"fs_s{i}", [128, T], F32, es) for i in range(2)]
    o_t = [P.sbuf(f"fs_o{i}", [128, T], F32, es) for i in range(2)]
    marks = [[0, 0], [0, 0]]

    def final(mc, ts, acc_ap):
        if ts != 0:
            return
        b = mc % 2

        @blk.sync
        def _(q, mc=mc, b=b):
            q.wait_ge(dv.h, marks[b][0])
            ins = q.dma_start(out=s_t[b][:],
                              in_=sin_dram[128 * mc:128 * (mc + 1), :])
            sl.inc(ins, 16)
        lm = sl.v

        @blk.vector
        def _(v, acc_ap=acc_ap, b=b, lm=lm):
            v.wait_ge(sl.h, lm)
            v.wait_ge(ss.h, marks[b][1])
            ins = v.tensor_add(o_t[b][:], s_t[b][:], acc_ap)
            dv.inc(ins, 1)
        marks[b][0] = dv.v

        @blk.scalar
        def _(q, mc=mc, b=b):
            q.wait_ge(dv.h, marks[b][0])
            ins = q.dma_start(out=sout_dram[128 * mc:128 * (mc + 1), :],
                              in_=o_t[b][:])
            ss.inc(ins, 16)
        marks[b][1] = ss.v
    return final


def emit_c(P, io):
    es = contextlib.ExitStack()
    y2 = P.sbuf("c_y2", [128, 16, T], BF16, es)
    emit_stream_ln(P, es, io["S1"], y2, "cln")
    es2 = contextlib.ExitStack()
    emit_proj2(P, es2, lambda k: y2[:, k, :], HID0, io["W1Q"], io["W2Q"],
               AF.Sigmoid, io["HID"], "ch")
    es2.close()
    es.close()
    P.fence()
    es3 = contextlib.ExitStack()
    fin = _final_stream_add(P, io["S1"], io["S2"], es3)
    emit_down(P, es3, io["HID"], HID0 // 128, io["W3Q"], False, F, fin, "cd")
    es3.close()
    P.fence()


def emit_e(P, io):
    es = contextlib.ExitStack()
    y4 = P.sbuf("e_y4", [128, 16, T], BF16, es)
    emit_stream_ln(P, es, io["S3"], y4, "eln")
    es2 = contextlib.ExitStack()
    emit_proj2(P, es2, lambda k: y4[:, k, :], MLP1, io["WGQ"], io["WUQ"],
               AF.Gelu_apprx_tanh, io["INTER"], "eh")
    es2.close()
    es.close()
    P.fence()
    es3 = contextlib.ExitStack()
    fin = _final_stream_add(P, io["S3"], io["OUT"], es3)
    emit_down(P, es3, io["INTER"], MLP1 // 128, io["WDQ"], False, F, fin, "ed")
    es3.close()
    P.fence()



def emit_d(P, io):
    nc, blk = P.nc, P.block
    sl, sw, ss = P.sems["sl"], P.sems["sw"], P.sems["ss"]
    pe, dv, ac = P.sems["pe"], P.sems["dv"], P.sems["ac"]
    ccs = P.sems["cc"]
    eps_r = P.constcol(EPSR)
    mhalf = P.constcol(-0.5)

    esQ = contextlib.ExitStack()
    idt = P.sbuf("d_idt", [128, 128], BF16, esQ)
    es = contextlib.ExitStack()
    y3 = P.sbuf("d_y3", [128, 16, T], BF16, es)
    emit_stream_ln(P, es, io["S2"], y3, "dln")

    es1 = contextlib.ExitStack()
    wqb = P.sbuf("d_wq", [128, 16, 2048], BF16, es1)
    wvb = P.sbuf("d_wv", [128, 16, 512], BF16, es1)
    qsq = P.sbuf("d_qsq", [128, 2048], F32, es1)
    qred = P.sbuf("d_qred", [128, 16], F32, es1)
    qtile = P.sbuf("d_qtile", [128, 2048], BF16, es1)
    vtile = P.sbuf("d_vtile", [128, 512], F32, es1)
    ps_q = P.psum("d_psq", [128, 2048], F32, es1)
    ps_v = P.psum("d_psv", [128, 512], F32, es1)

    @blk.sync
    def _(q):
        ins = q.dma_start(out=idt[:], in_=io["ident"])
        sl.inc(ins, 16)

    @blk.gpsimd
    def _(g):
        for k in range(FCH):
            ins = g.dma_start(out=wqb[:, k, :],
                              in_=io["Wq"][128 * k:128 * (k + 1), :])
            sw.inc(ins, 16)
        for k in range(FCH):
            ins = g.dma_start(out=wvb[:, k, :],
                              in_=io["Wv1"][128 * k:128 * (k + 1), :])
            sw.inc(ins, 16)

    for tc in range(FCH):
        @blk.tensor
        def _(t, tc=tc):
            t.wait_ge(sw.h, sw.v)
            t.wait_ge(dv.h, dv.v)
            t.wait_ge(ss.h, ss.v)
            for dt in range(4):
                for k in range(FCH):
                    t.matmul(ps_q[:, 512 * dt:512 * (dt + 1)],
                             y3[:, k, 128 * tc:128 * (tc + 1)],
                             wqb[:, k, 512 * dt:512 * (dt + 1)],
                             start=(k == 0), stop=(k == FCH - 1))
            for k in range(FCH):
                ins2 = t.matmul(ps_v[:], y3[:, k, 128 * tc:128 * (tc + 1)],
                                wvb[:, k, :],
                                start=(k == 0), stop=(k == FCH - 1))
            pe.inc(ins2, 1)
        pm = pe.v

        @blk.vector
        def _(v, pm=pm):
            v.wait_ge(pe.h, pm)
            ins = v.tensor_copy(vtile[:], ps_v[:])
            dv.inc(ins, 1)
        cm = dv.v

        @blk.scalar
        def _(q, tc=tc, cm=cm, pm=pm):
            q.wait_ge(dv.h, cm)
            ins = q.dma_start(out=io["V_loc"][128 * tc:128 * (tc + 1), :],
                              in_=vtile[:])
            ss.inc(ins, 16)
            q.wait_ge(pe.h, pm)
            ins = q.activation(qsq[:], ps_q[:], AF.Square)
            ac.inc(ins, 1)

        @blk.vector
        def _(v):
            v.wait_ge(ac.h, ac.v)
            ins = v.tensor_reduce(qred[:],
                                  qsq[:].rearrange("p (h d) -> p h d", h=16),
                                  AX.X, ALU.add)
            dv.inc(ins, 1)
            v.wait_ge(dv.h, dv.v)
            ins = v.tensor_scalar(qred[:], qred[:], 1.0 / 128, None, ALU.mult)
            dv.inc(ins, 1)

        @blk.scalar
        def _(q):
            q.wait_ge(dv.h, dv.v)
            ins = q.activation(qred[:], qred[:], AF.Ln, bias=eps_r[:])
            ac.inc(ins, 1)
            q.wait_ge(ac.h, ac.v)
            ins = q.activation(qred[:], qred[:], AF.Exp, scale=mhalf[:])
            ac.inc(ins, 1)

        @blk.vector
        def _(v, tc=tc):
            v.wait_ge(ac.h, ac.v)
            v.wait_ge(ss.h, ss.v)
            for hh in range(16):
                ins = v.tensor_scalar(qtile[:, 128 * hh:128 * (hh + 1)],
                                      ps_q[:, 128 * hh:128 * (hh + 1)],
                                      qred[:, hh:hh + 1], None, ALU.mult)
            dv.inc(ins, 1)
        qm = dv.v

        @blk.scalar
        def _(q, tc=tc, qm=qm):
            q.wait_ge(dv.h, qm)
            ins = q.dma_start(out=io["QBUF"][128 * tc:128 * (tc + 1), :],
                              in_=qtile[:])
            ss.inc(ins, 16)

    es1.close()
    P.fence()
    es.close()
    P.fence()

    @blk.gpsimd
    def _(g):
        ins = g.collective_compute(
            "AllGather", ALU.bypass,
            replica_groups=[[0, 1], [2, 3], [4, 5], [6, 7]],
            ins=[io["V_loc"].opt()], outs=[io["V_full"].opt()])
        ccs.inc(ins, 1)
        g.wait_ge(ccs.h, ccs.v)

    es2 = contextlib.ExitStack()
    vbf = P.sbuf("d_vbf", [128, 32, 512], BF16, es2)
    kfm = P.sbuf("d_kfm", [128, 4, 4096], BF16, es2)
    vf32 = P.sbuf("d_vf32", [128, 512], F32, es2)
    vsq = P.sbuf("d_vsq", [128, 512], F32, es2)
    kred = P.sbuf("d_kred", [128, 4], F32, es2)
    ktmb = P.sbuf("d_ktmb", [128, 512], BF16, es2)
    ps_tr = P.psum("d_pstr", [128, 1024], BF16, es2)
    vrow = 0

    @blk.gpsimd
    def _(g):
        for sc in range(32):
            ins = g.dma_start(
                out=vbf[:, sc, :],
                in_=io["V_full"][vrow + 128 * sc:vrow + 128 * (sc + 1), :])
            sw.inc(ins, 16)

    ev_mark = [0]
    for sc in range(32):
        @blk.sync
        def _(q, sc=sc):
            q.wait_ge(ccs.h, ccs.v)
            q.wait_ge(pe.h, pe.v)
            ins = q.dma_start(
                out=vf32[:],
                in_=io["V_full"][vrow + 128 * sc:vrow + 128 * (sc + 1), :])
            sl.inc(ins, 16)
        lm = sl.v

        @blk.scalar
        def _(q, lm=lm):
            q.wait_ge(sl.h, lm)
            q.wait_ge(dv.h, dv.v)
            ins = q.activation(vsq[:], vf32[:], AF.Square)
            ac.inc(ins, 1)

        @blk.vector
        def _(v):
            v.wait_ge(ac.h, ac.v)
            ins = v.tensor_reduce(kred[:],
                                  vsq[:].rearrange("p (h d) -> p h d", h=4),
                                  AX.X, ALU.add)
            dv.inc(ins, 1)
            v.wait_ge(dv.h, dv.v)
            ins = v.tensor_scalar(kred[:], kred[:], 1.0 / 128, None, ALU.mult)
            dv.inc(ins, 1)

        @blk.scalar
        def _(q):
            q.wait_ge(dv.h, dv.v)
            ins = q.activation(kred[:], kred[:], AF.Ln, bias=eps_r[:])
            ac.inc(ins, 1)
            q.wait_ge(ac.h, ac.v)
            ins = q.activation(kred[:], kred[:], AF.Exp, scale=mhalf[:])
            ac.inc(ins, 1)

        @blk.vector
        def _(v):
            v.wait_ge(ac.h, ac.v)
            for hh in range(4):
                ins = v.tensor_scalar(ktmb[:, 128 * hh:128 * (hh + 1)],
                                      vf32[:, 128 * hh:128 * (hh + 1)],
                                      kred[:, hh:hh + 1], None, ALU.mult)
            dv.inc(ins, 1)
        km = dv.v

        @blk.tensor
        def _(t, km=km):
            t.wait_ge(dv.h, km)
            t.wait_ge(sl.h, sl.v)
            t.wait_ge(dv.h, ev_mark[0])
            for hh in range(4):
                ins = t.transpose(ps_tr[:, 128 * hh:128 * (hh + 1)],
                                  ktmb[:, 128 * hh:128 * (hh + 1)], idt[:])
            pe.inc(ins, 1)
        pm = pe.v

        @blk.vector
        def _(v, sc=sc, pm=pm):
            v.wait_ge(pe.h, pm)
            for hh in range(4):
                ins = v.tensor_copy(kfm[:, hh, 128 * sc:128 * (sc + 1)],
                                    ps_tr[:, 128 * hh:128 * (hh + 1)])
            dv.inc(ins, 1)
            ev_mark[0] = dv.v

    es3 = contextlib.ExitStack()
    qbfb = P.sbuf("d_qbfb", [128, 16, 2048], BF16, es3)
    qfm = P.sbuf("d_qfm", [128, 2048], BF16, es3)
    pbuf = P.sbuf("d_pbuf", [128, 4096], BF16, es3)
    ptb = P.sbuf("d_ptb", [128, 32, 512], BF16, es3)
    mx4p = [P.sbuf(f"d_mx4{i}", [128, 8], F32, es3) for i in range(2)]
    esump = [P.sbuf(f"d_esum{i}", [128, 8], F32, es3) for i in range(2)]
    mone = P.constcol(-1.0)
    ytile = P.sbuf("d_ytile", [128, 512], BF16, es3)
    ps_s = P.psum("d_pss", [128, 2048], F32, es3)
    ps_y = P.psum("d_psy", [128, 512], F32, es3)

    @blk.sync
    def _(q):
        for tc in range(FCH):
            ins = q.dma_start(out=qbfb[:, tc, :],
                              in_=io["QBUF"][128 * tc:128 * (tc + 1), :])
            sl.inc(ins, 16)

    ptr_ev = [0]
    y_ev = [0]
    bank_mx = [0, 0]
    bank_ex = [0, 0]
    for qh in range(16):
        kv = qh // 4
        for g8 in range(2):
            @blk.tensor
            def _(t, qh=qh, g8=g8):
                t.wait_ge(sl.h, sl.v)
                t.wait_ge(dv.h, dv.v)
                for j in range(8):
                    ins = t.transpose(ps_tr[:, 128 * j:128 * (j + 1)],
                                      qbfb[:, 8 * g8 + j,
                                           128 * qh:128 * (qh + 1)], idt[:])
                pe.inc(ins, 1)
            pmq = pe.v

            @blk.vector
            def _(v, g8=g8, pmq=pmq):
                v.wait_ge(pe.h, pmq)
                ins = v.tensor_copy(qfm[:, 1024 * g8:1024 * (g8 + 1)],
                                    ps_tr[:])
                dv.inc(ins, 1)
                ptr_ev[0] = dv.v

        for tm in range(4):
            for tq in range(4):
                tc = 4 * tm + tq
                par = tc % 2
                mx4 = mx4p[par]
                esum = esump[par]
                for quar in range(4):
                    qb = quar % 2

                    @blk.tensor
                    def _(t, qh=qh, tc=tc, quar=quar, qb=qb):
                        t.wait_ge(dv.h, bank_mx[qb])
                        t.wait_ge(ac.h, bank_ex[qb])
                        for half in range(2):
                            ins = t.matmul(
                                ps_s[:, 1024 * qb + 512 * half:
                                     1024 * qb + 512 * (half + 1)],
                                qfm[:, 128 * tc:128 * (tc + 1)],
                                kfm[:, qh // 4,
                                    1024 * quar + 512 * half:
                                    1024 * quar + 512 * (half + 1)],
                                start=True, stop=True)
                        pe.inc(ins, 1)
                    pmS = pe.v

                    @blk.vector
                    def _(v, quar=quar, qb=qb, pmS=pmS, mx4=mx4):
                        v.wait_ge(pe.h, pmS)
                        ins = v.tensor_reduce(mx4[:, 4 + quar:5 + quar],
                                              ps_s[:, 1024 * qb:1024 * (qb + 1)],
                                              AX.X, ALU.max, negate=True)
                        dv.inc(ins, 1)
                    dmS = dv.v
                    bank_mx[qb] = dmS

                    @blk.scalar
                    def _(q, quar=quar, qb=qb, dmS=dmS, mx4=mx4, esum=esum):
                        q.wait_ge(dv.h, dmS)
                        ins = q.activation(
                            pbuf[:, 1024 * quar:1024 * (quar + 1)],
                            ps_s[:, 1024 * qb:1024 * (qb + 1)],
                            AF.Exp, bias=mx4[:, 4 + quar:5 + quar],
                            accum_out=esum[:, quar:quar + 1])
                        ac.inc(ins, 1)
                    bank_ex[qb] = ac.v

                @blk.vector
                def _(v, mx4=mx4):
                    v.wait_ge(ac.h, ac.v)
                    ins = v.tensor_reduce(mx4[:, 0:1], mx4[:, 4:8],
                                          AX.X, ALU.min)
                    dv.inc(ins, 1)
                    v.wait_ge(dv.h, dv.v)
                    ins = v.tensor_scalar(mx4[:, 4:8], mx4[:, 4:8],
                                          mx4[:, 0:1], None, ALU.subtract)
                    dv.inc(ins, 1)
                dmF = dv.v

                @blk.scalar
                def _(q, dmF=dmF, mx4=mx4):
                    q.wait_ge(dv.h, dmF)
                    ins = q.activation(mx4[:, 0:4], mx4[:, 4:8], AF.Exp,
                                       scale=mone[:])
                    ac.inc(ins, 1)

                @blk.vector
                def _(v, mx4=mx4, esum=esum):
                    v.wait_ge(ac.h, ac.v)

                    def step(ins):
                        dv.inc(ins, 1)
                        v.wait_ge(dv.h, dv.v)
                    step(v.tensor_mul(esum[:, 4:8], esum[:, 0:4], mx4[:, 0:4]))
                    step(v.tensor_reduce(esum[:, 0:1], esum[:, 4:8],
                                         AX.X, ALU.add))
                    step(v.reciprocal(esum[:, 1:2], esum[:, 0:1]))
                    step(v.tensor_scalar(mx4[:, 0:4], mx4[:, 0:4],
                                         esum[:, 1:2], None, ALU.mult))
                    for quar in range(4):
                        ins = v.tensor_scalar(
                            pbuf[:, 1024 * quar:1024 * (quar + 1)],
                            pbuf[:, 1024 * quar:1024 * (quar + 1)],
                            mx4[:, quar:quar + 1], None, ALU.mult)
                    dv.inc(ins, 1)
                pfix = dv.v
                if qh == 0 and tc == 0 and "DBGP" in io:
                    @blk.scalar
                    def _(q):
                        q.wait_ge(dv.h, dv.v)
                        ins = q.dma_start(out=io["DBGP"], in_=pbuf[:])
                        ss.inc(ins, 16)
                        ins = q.dma_start(out=io["DBGMX"], in_=mx4[:])
                        ss.inc(ins, 16)

                for g8 in range(4):
                    @blk.tensor
                    def _(t, g8=g8, pfix=pfix):
                        t.wait_ge(dv.h, pfix)
                        t.wait_ge(dv.h, ptr_ev[0])
                        for j in range(8):
                            ins = t.transpose(
                                ps_tr[:, 128 * j:128 * (j + 1)],
                                pbuf[:, 128 * (8 * g8 + j):
                                     128 * (8 * g8 + j + 1)], idt[:])
                        pe.inc(ins, 1)
                    pmT = pe.v

                    @blk.vector
                    def _(v, g8=g8, tq=tq, pmT=pmT):
                        v.wait_ge(pe.h, pmT)
                        for j in range(8):
                            ins = v.tensor_copy(
                                ptb[:, 8 * g8 + j, 128 * tq:128 * (tq + 1)],
                                ps_tr[:, 128 * j:128 * (j + 1)])
                        dv.inc(ins, 1)
                        ptr_ev[0] = dv.v

            @blk.tensor
            def _(t, kv=kv):
                t.wait_ge(dv.h, dv.v)
                t.wait_ge(ac.h, ac.v)
                for sc in range(32):
                    ins = t.matmul(ps_y[:], vbf[:, sc, 128 * kv:128 * (kv + 1)],
                                   ptb[:, sc, :],
                                   start=(sc == 0), stop=(sc == 31))
                pe.inc(ins, 1)
            pmY = pe.v

            @blk.scalar
            def _(q, qh=qh, tm=tm, pmY=pmY):
                q.wait_ge(pe.h, pmY)
                q.wait_ge(ss.h, y_ev[0])
                ins = q.activation(ytile[:], ps_y[:], AF.Identity)
                ac.inc(ins, 1)
                q.wait_ge(ac.h, ac.v)
                ins = q.dma_start(
                    out=io["Y"][128 * qh:128 * (qh + 1),
                                512 * tm:512 * (tm + 1)],
                    in_=ytile[:])
                ss.inc(ins, 16)
                y_ev[0] = ss.v

    es3.close()
    es2.close()
    esQ.close()
    P.fence()

    es4 = contextlib.ExitStack()
    ybuf = P.sbuf("d_ybuf", [128, 16, 2048], BF16, es4)
    wob = P.sbuf("d_wob", [128, 16, 2048], BF16, es4)
    zsq = P.sbuf("d_zsq", [128, 2048], F32, es4)
    zred = P.sbuf("d_zred", [128, 1], F32, es4)
    zbf = P.sbuf("d_zbf", [128, 2048], BF16, es4)
    afm = P.sbuf("d_afm", [128, 16, 128], BF16, es4)
    idt2 = P.sbuf("d_idt2", [128, 128], BF16, es4)
    ps_z = P.psum("d_psz", [128, 2048], F32, es4)
    ps_t2 = P.psum("d_pst2", [128, 1024], BF16, es4)

    @blk.sync
    def _(q):
        ins = q.dma_start(out=idt2[:], in_=io["ident"])
        sl.inc(ins, 16)
        for k in range(FCH):
            ins = q.dma_start(out=ybuf[:, k, :],
                              in_=io["Y"][128 * k:128 * (k + 1), :])
            sl.inc(ins, 16)

    @blk.gpsimd
    def _(g):
        for k in range(FCH):
            ins = g.dma_start(out=wob[:, k, :],
                              in_=io["Wo1"][128 * k:128 * (k + 1), :])
            sw.inc(ins, 16)

    att_ev = [0]
    for tc in range(FCH):
        @blk.tensor
        def _(t, tc=tc):
            t.wait_ge(sl.h, sl.v)
            t.wait_ge(sw.h, sw.v)
            t.wait_ge(ac.h, ac.v)
            t.wait_ge(dv.h, dv.v)
            for mt in range(4):
                for k in range(FCH):
                    ins = t.matmul(ps_z[:, 512 * mt:512 * (mt + 1)],
                                   ybuf[:, k, 128 * tc:128 * (tc + 1)],
                                   wob[:, k, 512 * mt:512 * (mt + 1)],
                                   start=(k == 0), stop=(k == FCH - 1))
            pe.inc(ins, 1)
        pmZ = pe.v

        @blk.scalar
        def _(q, pmZ=pmZ):
            q.wait_ge(pe.h, pmZ)
            q.wait_ge(dv.h, dv.v)
            ins = q.activation(zsq[:], ps_z[:], AF.Square)
            ac.inc(ins, 1)

        @blk.vector
        def _(v):
            v.wait_ge(ac.h, ac.v)
            ins = v.tensor_reduce(zred[:], zsq[:], AX.X, ALU.add)
            dv.inc(ins, 1)
            v.wait_ge(dv.h, dv.v)
            ins = v.tensor_scalar(zred[:], zred[:], 1.0 / F, None, ALU.mult)
            dv.inc(ins, 1)

        @blk.scalar
        def _(q):
            q.wait_ge(dv.h, dv.v)
            ins = q.activation(zred[:], zred[:], AF.Ln, bias=eps_r[:])
            ac.inc(ins, 1)
            q.wait_ge(ac.h, ac.v)
            ins = q.activation(zred[:], zred[:], AF.Exp, scale=mhalf[:])
            ac.inc(ins, 1)

        @blk.vector
        def _(v):
            v.wait_ge(ac.h, ac.v)
            ins = v.tensor_scalar(zbf[:], ps_z[:], zred[:, 0:1], None, ALU.mult)
            dv.inc(ins, 1)
        zm = dv.v

        for g8 in range(2):
            @blk.tensor
            def _(t, zm=zm, g8=g8):
                t.wait_ge(dv.h, zm)
                t.wait_ge(dv.h, dv.v)
                t.wait_ge(ss.h, att_ev[0])
                for j in range(8):
                    ins = t.transpose(ps_t2[:, 128 * j:128 * (j + 1)],
                                      zbf[:, 128 * (8 * g8 + j):
                                          128 * (8 * g8 + j + 1)], idt2[:])
                pe.inc(ins, 1)
            pmT = pe.v

            @blk.vector
            def _(v, g8=g8, pmT=pmT):
                v.wait_ge(pe.h, pmT)
                for j in range(8):
                    ins = v.tensor_copy(afm[:, 8 * g8 + j, :],
                                        ps_t2[:, 128 * j:128 * (j + 1)])
                dv.inc(ins, 1)

        @blk.scalar
        def _(q, tc=tc):
            q.wait_ge(dv.h, dv.v)
            ins = q.dma_start(
                out=io["ATT"][:, 128 * tc:128 * (tc + 1)].rearrange(
                    "(c p) t -> p c t", p=128),
                in_=afm[:])
            ss.inc(ins, 16)
            att_ev[0] = ss.v

    es4.close()
    P.fence()

    es5 = contextlib.ExitStack()
    s2t = P.sbuf("d_s2t", [128, 512], F32, es5)
    att_t = P.sbuf("d_attt", [128, 512], BF16, es5)
    s3t = P.sbuf("d_s3t", [128, 512], F32, es5)
    for mc in range(FCH):
        for ts in range(4):
            @blk.sync
            def _(q, mc=mc, ts=ts):
                q.wait_ge(dv.h, dv.v)
                ins = q.dma_start(out=s2t[:],
                                  in_=io["S2"][128 * mc:128 * (mc + 1),
                                               512 * ts:512 * (ts + 1)])
                sl.inc(ins, 16)
                ins = q.dma_start(out=att_t[:],
                                  in_=io["ATT"][128 * mc:128 * (mc + 1),
                                                512 * ts:512 * (ts + 1)])
                sl.inc(ins, 16)

            @blk.vector
            def _(v):
                v.wait_ge(sl.h, sl.v)
                v.wait_ge(ss.h, ss.v)
                ins = v.tensor_add(s3t[:], s2t[:], att_t[:])
                dv.inc(ins, 1)

            @blk.scalar
            def _(q, mc=mc, ts=ts):
                q.wait_ge(dv.h, dv.v)
                ins = q.dma_start(out=io["S3"][128 * mc:128 * (mc + 1),
                                               512 * ts:512 * (ts + 1)],
                                  in_=s3t[:])
                ss.inc(ins, 16)
    es5.close()
    P.fence()



def _run(P, in_maps):
    res = run_bass_kernel_spmd(P.nc, in_maps, core_ids=list(range(NC)))
    P.close()
    return res.results


def _build_prep():
    P = Prog()
    io = {}
    for n, (s, d) in {
        "w1T_s": ([256, HID0], F32), "w2T_s": ([256, HID0], F32),
        "w3T_s": ([704, F], F32),
        "wgT_s": ([128, MLP1], U8), "wuT_s": ([128, MLP1], U8),
        "wdT_s": ([1024, F], U8),
        "gsT_s": ([8, MLP1], F32), "usT_s": ([8, MLP1], F32),
        "dsT_s": ([64, F], F32),
    }.items():
        io[n] = P.dram(n, s, d, kind="ExternalInput")
    for n, s in (("gpart", [1, 4]), ("gall", [8, 4]), ("ginv", [1, 4])):
        io[n] = P.dram(n, s, F32)
    for n, s in {
        "w1q_s": [256, HID0], "w2q_s": [256, HID0], "w3q_s": [704, F],
        "wgq_s": [256, MLP1], "wuq_s": [256, MLP1], "wdq_s": [2048, F],
    }.items():
        io[n] = P.dram(n, s, BF16, kind="ExternalOutput")
    P.start_block()
    emit_prep(P, io, gather=False)

    blk, ss = P.block, P.sems["ss"]

    @blk.gpsimd
    def _(g):
        for s in P.sems.values():
            if s.v:
                g.wait_ge(s.h, s.v)
    return P


def _build_b01():
    P = Prog()
    io = {}
    io["x0"] = P.dram("x0", [F, 2050], F32, kind="ExternalInput")
    io["xtail"] = P.dram("xtail", [F, 1024], F32, kind="ExternalInput")
    io["tpos"] = P.dram("tpos", [1, 1024], F32, kind="ExternalInput")
    io["maskm"] = P.dram("maskm", [128, 1], F32, kind="ExternalInput")
    for nm in ("Wv", "Wr", "Wo"):
        io[nm] = P.dram(nm, [F, F], F32, kind="ExternalInput")
    for nm in ("ln1_g", "ln1_b", "tm_g", "tm_b", "tf", "td", "bo"):
        io[nm] = P.dram(nm, [128, 16], F32, kind="ExternalInput")
    io["X2"] = P.dram("X2", [F, T + 1], F32, kind="ExternalOutput")
    io["S1P"] = P.dram("S1P", [F, T], F32, kind="ExternalOutput")
    P.start_block()
    emit_b01(P, io)
    return P


def _build_b2():
    P = Prog()
    io = {}
    io["X2"] = P.dram("X2", [F, T + 1], F32, kind="ExternalInput")
    io["S1P"] = P.dram("S1P", [F, T], F32, kind="ExternalInput")
    io["maskm"] = P.dram("maskm", [128, 1], F32, kind="ExternalInput")
    for nm in ("Wcr",):
        io[nm] = P.dram(nm, [F, F], F32, kind="ExternalInput")
    io["Wck"] = P.dram("Wck", [F, I0], F32, kind="ExternalInput")
    io["Wcv"] = P.dram("Wcv", [I0, F], F32, kind="ExternalInput")
    for nm in ("ln2_g", "ln2_b", "cm_g", "cm_b"):
        io[nm] = P.dram(nm, [128, 16], F32, kind="ExternalInput")
    io["KC"] = P.dram("KC", [I0, T], BF16)
    io["RC"] = P.dram("RC", [F, T], BF16)
    io["S1"] = P.dram("S1", [F, T], F32, kind="ExternalOutput")
    P.start_block()
    emit_b2(P, io)
    return P


def _build_c():
    P = Prog()
    io = {}
    io["S1"] = P.dram("S1", [F, T], F32, kind="ExternalInput")
    io["W1Q"] = P.dram("W1Q", [F, HID0], BF16, kind="ExternalInput")
    io["W2Q"] = P.dram("W2Q", [F, HID0], BF16, kind="ExternalInput")
    io["W3Q"] = P.dram("W3Q", [HID0, F], BF16, kind="ExternalInput")
    io["HID"] = P.dram("HID", [HID0, T], BF16)
    io["S2"] = P.dram("S2", [F, T], F32, kind="ExternalOutput")
    P.start_block()
    emit_c(P, io)
    return P


def _build_d():
    P = Prog()
    io = {}
    io["S2"] = P.dram("S2", [F, T], F32, kind="ExternalInput")
    io["Wq"] = P.dram("Wq", [F, F], F32, kind="ExternalInput")
    io["Wv1"] = P.dram("Wv1", [F, 512], F32, kind="ExternalInput")
    io["Wo1"] = P.dram("Wo1", [F, F], F32, kind="ExternalInput")
    io["ident"] = P.dram("ident", [128, 128], BF16, kind="ExternalInput")
    io["V_loc"] = P.dram("V_loc", [T, 512], F32)
    io["V_full"] = P.dram("V_full", [2 * T, 512], F32)
    io["QBUF"] = P.dram("QBUF", [T, F], BF16)
    io["Y"] = P.dram("Y", [F, T], BF16)
    io["ATT"] = P.dram("ATT", [F, T], BF16)
    io["S3"] = P.dram("S3", [F, T], F32, kind="ExternalOutput")
    P.start_block()
    emit_d(P, io)
    return P


def _build_e():
    P = Prog()
    io = {}
    io["S3"] = P.dram("S3", [F, T], F32, kind="ExternalInput")
    io["WGQ"] = P.dram("WGQ", [F, MLP1], BF16, kind="ExternalInput")
    io["WUQ"] = P.dram("WUQ", [F, MLP1], BF16, kind="ExternalInput")
    io["WDQ"] = P.dram("WDQ", [MLP1, F], BF16, kind="ExternalInput")
    io["INTER"] = P.dram("INTER", [MLP1, T], BF16)
    io["OUT"] = P.dram("OUT", [F, T], F32, kind="ExternalOutput")
    P.start_block()
    emit_e(P, io)
    return P


def _build_fused():
    P = Prog()
    io = {}
    for n, (s, d) in {
        "w1T_s": ([256, HID0], F32), "w2T_s": ([256, HID0], F32),
        "w3T_s": ([704, F], F32),
        "wgT_s": ([128, MLP1], U8), "wuT_s": ([128, MLP1], U8),
        "wdT_s": ([1024, F], U8),
        "gsT_s": ([8, MLP1], F32), "usT_s": ([8, MLP1], F32),
        "dsT_s": ([64, F], F32),
        "x0": ([F, 2050], F32), "xtail": ([F, 1024], F32),
        "tpos": ([1, 1024], F32), "maskm": ([128, 1], F32),
        "Wv": ([F, F], F32), "Wr": ([F, F], F32), "Wo": ([F, F], F32),
        "Wcr": ([F, F], F32), "Wck": ([F, I0], F32), "Wcv": ([I0, F], F32),
        "Wq": ([F, F], F32), "Wv1": ([F, 512], F32), "Wo1": ([F, F], F32),
        "ident": ([128, 128], BF16),
    }.items():
        io[n] = P.dram(n, s, d, kind="ExternalInput")
    for nm in ("ln1_g", "ln1_b", "tm_g", "tm_b", "tf", "td", "bo",
               "ln2_g", "ln2_b", "cm_g", "cm_b"):
        io[nm] = P.dram(nm, [128, 16], F32, kind="ExternalInput")
    for n, (s, d) in {
        "gpart": ([1, 4], F32), "gall": ([8, 4], F32), "ginv": ([1, 4], F32),
        "w1q_s": ([256, HID0], BF16), "w2q_s": ([256, HID0], BF16),
        "w3q_s": ([704, F], BF16), "wgq_s": ([256, MLP1], BF16),
        "wuq_s": ([256, MLP1], BF16), "wdq_s": ([2048, F], BF16),
        "W1Q": ([F, HID0], BF16), "W2Q": ([F, HID0], BF16),
        "W3Q": ([HID0, F], BF16), "WGQ": ([F, MLP1], BF16),
        "WUQ": ([F, MLP1], BF16), "WDQ": ([MLP1, F], BF16),
        "X2": ([F, T + 1], F32), "S1P": ([F, T], F32),
        "KC": ([I0, T], BF16), "RC": ([F, T], BF16), "S1": ([F, T], F32),
        "HID": ([HID0, T], BF16), "S2": ([F, T], F32),
        "V_loc": ([T, 512], F32), "V_full": ([2 * T, 512], F32),
        "QBUF": ([T, F], BF16), "Y": ([F, T], BF16), "ATT": ([F, T], BF16),
        "S3": ([F, T], F32), "INTER": ([MLP1, T], BF16),
    }.items():
        io[n] = P.dram(n, s, d)
    io["OUT"] = P.dram("OUT", [F, T], F32, kind="ExternalOutput")

    P.start_block()
    emit_prep(P, io, gather=True)
    emit_b01(P, io)
    emit_b2(P, io)
    emit_c(P, io)
    emit_d(P, io)
    emit_e(P, io)
    return P


FUSED = True
IN_KEYS = ["w1T_s", "w2T_s", "w3T_s", "wgT_s", "wuT_s", "wdT_s",
           "gsT_s", "usT_s", "dsT_s", "x0", "xtail", "tpos", "maskm",
           "Wv", "Wr", "Wo", "Wcr", "Wck", "Wcv", "Wq", "Wv1", "Wo1",
           "ident", "ln1_g", "ln1_b", "tm_g", "tm_b", "tf", "td", "bo",
           "ln2_g", "ln2_b", "cm_g", "cm_b"]


def kernel_fused(inputs):
    per = stage_inputs(inputs)
    r = _run(_build_fused(), [{k: m[k] for k in IN_KEYS} for m in per])
    return assemble_output([r[c]["OUT"] for c in range(NC)])


def kernel(**inputs):
    if FUSED:
        return kernel_fused(inputs)
    return kernel_unfused(inputs)


def kernel_unfused(inputs):
    per = stage_inputs(inputs)
    cat = np.concatenate

    keys = ["w1T_s", "w2T_s", "w3T_s", "wgT_s", "wuT_s", "wdT_s",
            "gsT_s", "usT_s", "dsT_s"]
    r1 = _run(_build_prep(), [{k: m[k] for k in keys} for m in per])
    W1Q = cat([r1[c]["w1q_s"] for c in range(NC)], 0)
    W2Q = cat([r1[c]["w2q_s"] for c in range(NC)], 0)
    W3Q = cat([r1[c]["w3q_s"] for c in range(NC)], 0)
    WGQ = cat([r1[c]["wgq_s"] for c in range(NC)], 0)
    WUQ = cat([r1[c]["wuq_s"] for c in range(NC)], 0)
    WDQ = cat([r1[c]["wdq_s"] for c in range(NC)], 0)

    keys = ["x0", "xtail", "tpos", "maskm", "Wv", "Wr", "Wo",
            "ln1_g", "ln1_b", "tm_g", "tm_b", "tf", "td", "bo"]
    r2 = _run(_build_b01(), [{k: m[k] for k in keys} for m in per])

    keys = ["maskm", "Wcr", "Wck", "Wcv", "ln2_g", "ln2_b", "cm_g", "cm_b"]
    r3 = _run(_build_b2(), [
        dict({k: m[k] for k in keys}, X2=r2[c]["X2"], S1P=r2[c]["S1P"])
        for c, m in enumerate(per)])

    r4 = _run(_build_c(), [
        {"S1": r3[c]["S1"], "W1Q": W1Q, "W2Q": W2Q, "W3Q": W3Q}
        for c in range(NC)])

    keys = ["Wq", "Wv1", "Wo1", "ident"]
    r5 = _run(_build_d(), [
        dict({k: m[k] for k in keys}, S2=r4[c]["S2"])
        for c, m in enumerate(per)])

    r6 = _run(_build_e(), [
        {"S3": r5[c]["S3"], "WGQ": WGQ, "WUQ": WUQ, "WDQ": WDQ}
        for c in range(NC)])

    return assemble_output([r6[c]["OUT"] for c in range(NC)])


# version 06 (final)
# speedup vs baseline: 1.0070x; 1.0070x over previous
import contextlib
import numpy as np
import ml_dtypes

import concourse.bass as bass
import concourse.mybir as mybir
from concourse.bass_utils import run_bass_kernel_spmd

F32 = mybir.dt.float32
BF16 = mybir.dt.bfloat16
I16 = mybir.dt.int16
I32 = mybir.dt.int32
U8 = mybir.dt.uint8
AF = mybir.ActivationFunctionType
ALU = mybir.AluOpType
AX = mybir.AxisListType

NC = 8
T = 2048
F = 2048
FCH = 16
TAIL = 1024
HID0 = 5632
I0 = 8192
MLP1 = 16384
EPS0 = 1e-5
EPSL = 1e-12
EPSR = 1.1920929e-07

bf = ml_dtypes.bfloat16



def make_pi():
    r = np.arange(F)
    c = r // 256
    j = r % 256
    return np.where(j < 128, 2 * (128 * c + j), 2 * (128 * c + j - 128) + 1)


def make_pi2():
    s = np.arange(MLP1)
    c = s // 2048
    j = s % 2048
    return np.where(j < 1024, 2 * (1024 * c + j), 2 * (1024 * c + j - 1024) + 1)


PI = make_pi()
PI2 = make_pi2()


def _u8view(a_int32):
    a = np.ascontiguousarray(a_int32)
    return a.view(np.uint8)[..., 0::4] if a.dtype.itemsize == 4 else a


def _vec(v):
    return np.ascontiguousarray(np.asarray(v, np.float32)[PI].reshape(FCH, 128).T)


def stage_inputs(inp):
    x = np.asarray(inp["L0_attn_x"], np.float32)
    per = []
    c = np.ascontiguousarray

    Wv = c(np.asarray(inp["L0_attn_Wv"], np.float32)[PI][:, PI])
    Wr = c(np.asarray(inp["L0_attn_Wr"], np.float32)[PI][:, PI])
    Wo = c(np.asarray(inp["L0_attn_Wo"], np.float32)[PI][:, PI])
    Wcr = c(np.asarray(inp["L0_attn_Wcr"], np.float32)[PI][:, PI])
    Wck = c(np.asarray(inp["L0_attn_Wck"], np.float32)[PI, :])
    Wcv = c(np.asarray(inp["L0_attn_Wcv"], np.float32)[:, PI])
    Wq = c(np.asarray(inp["L1_attn_Wq"], np.float32)[PI, :])
    Wv1 = c(np.asarray(inp["L1_attn_Wv"], np.float32)[PI, :])
    Wo1 = c(np.asarray(inp["L1_attn_Wo"], np.float32)[:, PI])

    w1T = np.asarray(inp["L0_ffn_w1"], np.float32).T[PI, :]
    w2T = np.asarray(inp["L0_ffn_w2"], np.float32).T[PI, :]
    w3T = np.asarray(inp["L0_ffn_w3"], np.float32).T[:, PI]

    wgT = _u8view(np.asarray(inp["L1_ffn_gate_W_packed"])).T[:, PI2]
    wuT = _u8view(np.asarray(inp["L1_ffn_up_W_packed"])).T[:, PI2]
    wdT = _u8view(np.asarray(inp["L1_ffn_down_W_packed"])).T[:, PI]
    gsT = np.asarray(inp["L1_ffn_gate_scales"], np.float32).T[:, PI2]
    usT = np.asarray(inp["L1_ffn_up_scales"], np.float32).T[:, PI2]
    dsT = np.asarray(inp["L1_ffn_down_scales"], np.float32).T[:, PI]

    vecs = {
        "ln1_g": _vec(inp["L0_attn_ln1_g"]), "ln1_b": _vec(inp["L0_attn_ln1_b"]),
        "ln2_g": _vec(inp["L0_attn_ln2_g"]), "ln2_b": _vec(inp["L0_attn_ln2_b"]),
        "tm_g": _vec(inp["L0_attn_tm_g"]), "tm_b": _vec(inp["L0_attn_tm_b"]),
        "cm_g": _vec(inp["L0_attn_cm_g"]), "cm_b": _vec(inp["L0_attn_cm_b"]),
        "tf": _vec(inp["L0_attn_time_first"]), "td": _vec(inp["L0_attn_time_decay"]),
        "bo": _vec(inp["L0_attn_bo"]),
    }
    tpos = np.arange(1023, -1, -1, dtype=np.float32).reshape(1, TAIL)
    ident = np.eye(128, dtype=bf)

    for core in range(NC):
        b, h = core // 2, core % 2
        xT = x[b].T[PI, :]
        t0 = h * T
        if h == 0:
            x0 = np.concatenate([np.zeros((F, 2), np.float32), xT[:, 0:T]], 1)
        else:
            x0 = xT[:, t0 - 2:t0 + T]
        m = {
            "x0": c(x0),
            "xtail": c(xT[:, 4096 - TAIL:]),
            "tpos": tpos,
            "maskm": np.full((128, 1), float(h == 1), np.float32),
            "ident": ident,
            "Wv": Wv, "Wr": Wr, "Wo": Wo, "Wcr": Wcr, "Wck": Wck, "Wcv": Wcv,
            "Wq": Wq, "Wv1": Wv1, "Wo1": Wo1,
            "w1T_s": c(w1T[256 * core:256 * (core + 1), :]),
            "w2T_s": c(w2T[256 * core:256 * (core + 1), :]),
            "w3T_s": c(w3T[704 * core:704 * (core + 1), :]),
            "wgT_s": c(wgT[128 * core:128 * (core + 1), :]),
            "wuT_s": c(wuT[128 * core:128 * (core + 1), :]),
            "wdT_s": c(wdT[1024 * core:1024 * (core + 1), :]),
            "gsT_s": c(gsT[8 * core:8 * (core + 1), :]),
            "usT_s": c(usT[8 * core:8 * (core + 1), :]),
            "dsT_s": c(dsT[64 * core:64 * (core + 1), :]),
        }
        m.update(vecs)
        per.append(m)
    return per


def assemble_output(shards):
    out = np.empty((4, 4096, 2048), np.float32)
    for core in range(NC):
        b, h = core // 2, core % 2
        blk = np.empty((T, F), np.float32)
        blk[:, PI] = shards[core].T
        out[b, h * T:(h + 1) * T, :] = blk
    return out



class Sem:
    __slots__ = ("h", "v", "name")

    def __init__(self, h, name):
        self.h, self.v, self.name = h, 0, name

    def inc(self, ins, n=1):
        ins.then_inc(self.h, n)
        self.v += n
        return self.v


class Prog:

    def __init__(self):
        self.nc = bass.Bass("TRN2", target_bir_lowering=False)
        self.es = contextlib.ExitStack()
        self.block = None
        self.sems = {}
        self._sb = {}
        self._consts = {}

    def constcol(self, val):
        key = float(val)
        if key not in self._consts:
            i = len(self._consts)
            col = self._cpool[:, i:i + 1]
            dv = self.sems["dv"]

            @self.block.vector
            def _(v):
                ins = v.memset(col, key)
                dv.inc(ins, 1)
            self._consts[key] = col
        return self._consts[key]

    def start_block(self):
        for name in ("sl", "sw", "ss", "pe", "dv", "ac", "po", "cc"):
            self.sems[name] = Sem(self.es.enter_context(self.nc.semaphore(name)), name)
        self._cpool = self.sbuf("const_pool", [128, 16], F32)
        self.ones_c = self.sbuf("g_ones_c", [128, 1], F32)
        self.ones_r = self.sbuf("g_ones_r", [1, 128], F32)
        self.block = self.es.enter_context(self.nc.Block())

        @self.block.vector
        def _(v):
            v.memset(self.ones_c[:], 1.0)
            ins = v.memset(self.ones_r[:], 1.0)
            self.sems["dv"].inc(ins, 1)

    _uid = 0

    def sbuf(self, name, shape, dtype, es=None):
        if name in self._sb:
            Prog._uid += 1
            name = f"{name}_{Prog._uid}"
        t = (es or self.es).enter_context(self.nc.sbuf_tensor(name, shape, dtype))
        self._sb[name] = t
        return t

    def fence(self):
        blk = self.block
        sems = [s for s in self.sems.values() if s.v]
        for eng in ("sync", "scalar", "vector", "tensor", "gpsimd"):
            @getattr(blk, eng)
            def _(q):
                for s in sems:
                    q.wait_ge(s.h, s.v)

    def psum(self, name, shape, dtype=F32, es=None):
        if name in self._sb:
            Prog._uid += 1
            name = f"{name}_{Prog._uid}"
        self._sb[name] = True
        return (es or self.es).enter_context(self.nc.psum_tensor(name, shape, dtype))

    def dram(self, name, shape, dtype, kind=None):
        if kind is None:
            return self.nc.dram_tensor(name, shape, dtype).ap()
        return self.nc.dram_tensor(name, shape, dtype, kind=kind).ap()

    def close(self):
        self.es.close()



NW1 = float(HID0 * F)


def emit_prep(P, io, gather):
    nc, blk = P.nc, P.block
    sl, sw, ss = P.sems["sl"], P.sems["sw"], P.sems["ss"]
    pe, dv, ac, po, ccs = (P.sems[k] for k in ("pe", "dv", "ac", "po", "cc"))

    es1 = contextlib.ExitStack()
    w1sb = P.sbuf("p_w1", [128, 2, HID0], F32, es1)
    w2sb = P.sbuf("p_w2", [128, 2, HID0], F32, es1)
    w3sb = P.sbuf("p_w3", [128, 6, F], F32, es1)
    red = P.sbuf("p_red", [128, 8], F32, es1)
    ones = P.sbuf("p_ones", [128, 1], F32, es1)
    gsb = P.sbuf("p_gsb", [128, 8], F32, es1)
    gal2 = P.sbuf("p_gal2", [128, 32], F32, es1)
    invg = P.sbuf("p_invg", [128, 4], F32, es1)
    qa = P.sbuf("p_qa", [128, HID0], F32, es1)
    qb = P.sbuf("p_qb", [128, HID0], F32, es1)
    qq = P.sbuf("p_qq", [128, HID0], BF16, es1)
    ps_g = P.psum("p_psg", [128, 8], F32, es1)

    @blk.sync
    def _(q):
        for ins in (
            q.dma_start(out=w1sb[:], in_=io["w1T_s"].rearrange("(c p) n -> p c n", p=128)),
            q.dma_start(out=w2sb[:], in_=io["w2T_s"].rearrange("(c p) n -> p c n", p=128)),
            q.dma_start(out=w3sb[:, 0:5, :],
                        in_=io["w3T_s"][0:640, :].rearrange("(c p) n -> p c n", p=128)),
            q.dma_start(out=w3sb[0:64, 5, :], in_=io["w3T_s"][640:704, :]),
        ):
            sl.inc(ins, 16)

    @blk.vector
    def _(v):
        v.memset(w3sb[64:128, 5, :], 0.0)
        v.memset(ones[:], 1.0)
        v.wait_ge(sl.h, sl.v)
        v.tensor_reduce(red[:, 0:1], w1sb[:].rearrange("p c n -> p (c n)"),
                        AX.X, ALU.add, apply_absolute_value=True)
        v.tensor_reduce(red[:, 1:2], w2sb[:].rearrange("p c n -> p (c n)"),
                        AX.X, ALU.add, apply_absolute_value=True)
        ins = v.tensor_reduce(red[:, 2:3], w3sb[:].rearrange("p c n -> p (c n)"),
                              AX.X, ALU.add, apply_absolute_value=True)
        v.memset(red[:, 3:4], 0.0)
        dv.inc(ins, 1)

    @blk.tensor
    def _(t):
        t.wait_ge(dv.h, dv.v)
        ins = t.matmul(ps_g[0:1, 0:4], ones[:], red[:, 0:4], start=True, stop=True)
        pe.inc(ins, 1)

    @blk.vector
    def _(v):
        v.wait_ge(pe.h, pe.v)
        ins = v.tensor_copy(gsb[0:1, 0:4], ps_g[0:1, 0:4])
        dv.inc(ins, 1)

    @blk.scalar
    def _(q):
        q.wait_ge(dv.h, dv.v)
        ins = q.dma_start(out=io["gpart"], in_=gsb[0:1, 0:4])
        ss.inc(ins, 16)

    @blk.gpsimd
    def _(g):
        g.wait_ge(ss.h, ss.v)
        ins = g.collective_compute(
            "AllGather", ALU.bypass, replica_groups=[list(range(NC))],
            ins=[io["gpart"].opt()], outs=[io["gall"].opt()])
        ccs.inc(ins, 1)
        g.wait_ge(ccs.h, ccs.v)
        ins = g.dma_start(
            out=gal2[0:1, 0:32],
            in_=io["gall"].rearrange("a b -> (a b)").unsqueeze(0))
        sw.inc(ins, 16)

    @blk.vector
    def _(v):
        v.wait_ge(sw.h, sw.v)
        ins = v.tensor_reduce(gsb[0:1, 4:8],
                              gal2[0:1, 0:32].rearrange("p (b a) -> p a b", a=4),
                              AX.X, ALU.add)
        dv.inc(ins, 1)
        v.wait_ge(dv.h, dv.v)
        ins = v.tensor_scalar(gsb[0:1, 0:4], gsb[0:1, 4:8], 1.0 / NW1, 1e-8,
                              ALU.mult, ALU.add)
        dv.inc(ins, 1)
        v.wait_ge(dv.h, dv.v)
        ins = v.reciprocal(gsb[0:1, 4:8], gsb[0:1, 0:4])
        dv.inc(ins, 1)
        v.wait_ge(dv.h, dv.v)
        ins = v.tensor_copy(gsb[0:1, 0:4], gsb[0:1, 4:8])
        dv.inc(ins, 1)

    @blk.scalar
    def _(q):
        q.wait_ge(dv.h, dv.v)
        ins = q.dma_start(out=io["ginv"], in_=gsb[0:1, 0:4])
        ss.inc(ins, 16)

    @blk.sync
    def _(q):
        q.wait_ge(ss.h, ss.v)
        ins = q.dma_start(out=invg[:], in_=io["ginv"].broadcast_to([128, 4]))
        sl.inc(ins, 16)

    def qchunk(src_c, col, rows, width, dst):
        @blk.vector
        def _(v):
            v.wait_ge(sl.h, sl.v)
            v.wait_ge(ss.h, ss.v)
            v.tensor_scalar(qa[:, 0:width], src_c, invg[:, col:col + 1], 0.5,
                            ALU.mult, ALU.is_gt)
            v.tensor_scalar(qb[:, 0:width], src_c, invg[:, col:col + 1], -0.5,
                            ALU.mult, ALU.is_lt)
            ins = v.tensor_sub(qq[:, 0:width], qa[:, 0:width], qb[:, 0:width])
            dv.inc(ins, 1)
        mark = dv.v

        @blk.scalar
        def _(q):
            q.wait_ge(dv.h, mark)
            ins = q.dma_start(out=dst, in_=qq[0:rows, 0:width])
            ss.inc(ins, 16)

    for c in range(2):
        qchunk(w1sb[:, c, :], 0, 128, HID0,
               io["w1q_s"][c * 128:(c + 1) * 128, :])
    for c in range(2):
        qchunk(w2sb[:, c, :], 1, 128, HID0,
               io["w2q_s"][c * 128:(c + 1) * 128, :])
    for c in range(6):
        rows = min(128, 704 - c * 128)
        qchunk(w3sb[:, c, :], 2, rows, F,
               io["w3q_s"][c * 128:c * 128 + rows, :])

    es1.close()
    P.fence()
    _emit_decode(P, io)

    if gather:
        @blk.gpsimd
        def _(g):
            g.wait_ge(ss.h, ss.v)
            for a, b in (("w1q_s", "W1Q"), ("w2q_s", "W2Q"), ("w3q_s", "W3Q")):
                ins = g.collective_compute(
                    "AllGather", ALU.bypass, replica_groups=[list(range(NC))],
                    ins=[io[a].opt()], outs=[io[b].opt()])
                ccs.inc(ins, 1)
                g.wait_ge(ccs.h, ccs.v)


def _emit_decode(P, io):
    nc, blk = P.nc, P.block
    sl, ss, dv = P.sems["sl"], P.sems["ss"], P.sems["dv"]

    es2 = contextlib.ExitStack()
    u8t = P.sbuf("d_u8", [128, MLP1], I16, es2)
    sc = P.sbuf("d_sc", [128, 4096], F32, es2)
    t_lo = P.sbuf("d_lo", [128, 4096], I16, es2)
    t_hi = P.sbuf("d_hi", [128, 4096], I16, es2)
    t1 = P.sbuf("d_t1", [128, 4096], I16, es2)
    t2 = P.sbuf("d_t2", [128, 4096], I16, es2)
    t3 = P.sbuf("d_t3", [128, 4096], I16, es2)
    t4 = P.sbuf("d_t4", [128, 4096], I16, es2)
    vb = P.sbuf("d_vb", [128, 4096], BF16, es2)
    ob = P.sbuf("d_ob", [128, 4096], BF16, es2)

    def decode_plane(v, pl):
        v.tensor_scalar(t1[:], pl, 6, 6, ALU.bitwise_and, ALU.logical_shift_left)
        v.tensor_scalar(t2[:], pl, 8, 12, ALU.bitwise_and, ALU.logical_shift_left)
        v.tensor_scalar(t3[:], t1[:], 1, None, ALU.min)
        v.scalar_tensor_tensor(t1[:], t1[:], float(126 << 7), t2[:],
                               ALU.add, ALU.add)
        v.tensor_scalar(t2[:], pl, 7, None, ALU.bitwise_and)
        v.tensor_scalar(t2[:], t2[:], 1, None, ALU.min)
        v.tensor_scalar(t4[:], pl, 1, None, ALU.bitwise_and)
        v.scalar_tensor_tensor(t3[:], t3[:], 64.0, t4[:],
                               ALU.mult, ALU.mult)
        v.scalar_tensor_tensor(t2[:], t2[:], float(0x3F80), t3[:],
                               ALU.mult, ALU.add)
        v.tensor_tensor(vb[:], t1[:].bitcast(BF16), t2[:].bitcast(BF16),
                        ALU.mult)
        return v.tensor_tensor(ob[:], vb[:], sc[:], ALU.mult)

    def one_matrix(u8_dram, out_dram, npieces, width, scsrc):
        @blk.gpsimd
        def _(q):
            q.wait_ge(dv.h, dv.v)
            ins = q.dma_start(out=u8t[:, 0:npieces * width], in_=u8_dram)
            sl.inc(ins, 16)
        for p in range(npieces):
            @blk.sync
            def _(q):
                q.wait_ge(dv.h, dv.v)
                rows = scsrc(p)
                for r, a in enumerate(rows):
                    ins = q.dma_start(out=sc[16 * r:16 * (r + 1), 0:width],
                                      in_=a.broadcast_to([16, width]))
                    sl.inc(ins, 16)

            @blk.vector
            def _(v):
                v.wait_ge(sl.h, sl.v)
                v.wait_ge(ss.h, ss.v)
                v.tensor_scalar(t_lo[:, 0:width], u8t[:, p * width:(p + 1) * width],
                                15, None, ALU.bitwise_and)
                v.tensor_scalar(t_hi[:, 0:width], u8t[:, p * width:(p + 1) * width],
                                4, None, ALU.logical_shift_right)
                ins = decode_plane(v, t_lo[:, 0:width])
                dv.inc(ins, 1)
                P._dm = dv.v

            @blk.scalar
            def _(q):
                q.wait_ge(dv.h, P._dm)
                ins = q.dma_start(out=out_dram[0][:, p * width:(p + 1) * width],
                                  in_=ob[:, 0:width])
                ss.inc(ins, 16)

            @blk.vector
            def _(v):
                v.wait_ge(ss.h, ss.v)
                ins = decode_plane(v, t_hi[:, 0:width])
                dv.inc(ins, 1)
                P._dm = dv.v

            @blk.scalar
            def _(q):
                q.wait_ge(dv.h, P._dm)
                ins = q.dma_start(out=out_dram[1][:, p * width:(p + 1) * width],
                                  in_=ob[:, 0:width])
                ss.inc(ins, 16)

    for (wname, sname, oname) in (("wgT_s", "gsT_s", "wgq_s"),
                                  ("wuT_s", "usT_s", "wuq_s")):
        gdram = io[sname]

        def scsrc(p, gdram=gdram):
            return [gdram[r:r + 1, p * 4096:(p + 1) * 4096] for r in range(8)]
        one_matrix(io[wname], (io[oname][0:128, :], io[oname][128:256, :]),
                   4, 4096, scsrc)

    ddram = io["wdT_s"].rearrange("(g p) h -> p g h", p=128)
    dsd = io["dsT_s"]

    def dscsrc(p):
        return [(j, dsd[8 * g + r:8 * g + r + 1, :])
                for j, g in enumerate((2 * p, 2 * p + 1)) for r in range(8)]

    _one_matrix_down(P, blk, ddram, dscsrc, io["wdq_s"], u8t, sc, t_lo, t_hi,
                     decode_plane)
    es2.close()
    P.fence()


def _one_matrix_down(P, blk, ddram, dscsrc, out_dram, u8t, sc, t_lo, t_hi,
                     decode_plane):
    sl, ss, dv = P.sems["sl"], P.sems["ss"], P.sems["dv"]

    @blk.gpsimd
    def _(q):
        q.wait_ge(dv.h, dv.v)
        ins = q.dma_start(out=u8t[:].rearrange("p (g h) -> p g h", g=8),
                          in_=ddram)
        sl.inc(ins, 16)
    width = 4096
    for p in range(4):
        @blk.sync
        def _(q):
            q.wait_ge(dv.h, dv.v)
            items = dscsrc(p)
            for i, (j, a) in enumerate(items):
                r = i % 8
                ins = q.dma_start(
                    out=sc[16 * r:16 * (r + 1), j * 2048:(j + 1) * 2048],
                    in_=a.broadcast_to([16, 2048]))
                sl.inc(ins, 16)

        @blk.vector
        def _(v):
            v.wait_ge(sl.h, sl.v)
            v.wait_ge(ss.h, ss.v)
            v.tensor_scalar(t_lo[:, 0:width], u8t[:, p * width:(p + 1) * width],
                            15, None, ALU.bitwise_and)
            v.tensor_scalar(t_hi[:, 0:width], u8t[:, p * width:(p + 1) * width],
                            4, None, ALU.logical_shift_right)
            ins = decode_plane(v, t_lo[:, 0:width])
            dv.inc(ins, 1)
            P._dm = dv.v

        @blk.scalar
        def _(q):
            q.wait_ge(dv.h, P._dm)
            ob = P._sb["d_ob"]
            for j, g in enumerate((2 * p, 2 * p + 1)):
                ins = q.dma_start(
                    out=out_dram[g * 128:(g + 1) * 128, :],
                    in_=ob[:, j * 2048:(j + 1) * 2048])
                ss.inc(ins, 16)

        @blk.vector
        def _(v):
            v.wait_ge(ss.h, ss.v)
            ins = decode_plane(v, t_hi[:, 0:width])
            dv.inc(ins, 1)
            P._dm = dv.v

        @blk.scalar
        def _(q):
            q.wait_ge(dv.h, P._dm)
            ob = P._sb["d_ob"]
            for j, g in enumerate((2 * p, 2 * p + 1)):
                ins = q.dma_start(
                    out=out_dram[1024 + g * 128:1024 + (g + 1) * 128, :],
                    in_=ob[:, j * 2048:(j + 1) * 2048])
                ss.inc(ins, 16)



class LNKit:

    def __init__(self, P, es, W=512):
        self.W = W
        self.sq = [P.sbuf(f"ln_sq{i}", [128, 4, W], F32, es) for i in range(2)]
        self.tb = P.sbuf("ln_tb", [128, 16, W], F32, es)
        self.strow = P.sbuf("ln_st", [1, 6, W], F32, es)
        self.Rb = P.sbuf("ln_Rb", [128, W], F32, es)
        self.Mb = P.sbuf("ln_Mb", [128, W], F32, es)
        self.ps_sum = P.psum("ln_pssum", [1, W], F32, es)
        self.ps_sq = P.psum("ln_pssq", [1, W], F32, es)
        self.ps_R = P.psum("ln_psR", [128, W], F32, es)
        self.ps_M = P.psum("ln_psM", [128, W], F32, es)


def emit_ln(P, kit, src, dst, W, eps, gcol=None, bcol=None, inplace=False,
            srcb=None, dstb=None):
    blk = P.block
    pe, dv, ac = P.sems["pe"], P.sems["dv"], P.sems["ac"]
    ones_c, ones_r = P.ones_c, P.ones_r
    sq, strow, Rb, Mb = kit.sq, kit.strow, kit.Rb, kit.Mb

    ac0, pe0 = ac.v, pe.v
    sl, sw, dv0 = P.sems["sl"], P.sems["sw"], P.sems["dv"]
    for g in range(4):
        @blk.scalar
        def _(q, g=g):
            if g == 0:
                q.wait_ge(sl.h, sl.v)
                q.wait_ge(dv0.h, dv0.v)
            if g >= 2:
                q.wait_ge(pe.h, pe0 + 4 * (g - 1))
            for j in range(4):
                ins = q.activation(sq[g % 2][:, j, 0:W], src(4 * g + j),
                                   AF.Square)
            ac.inc(ins, 1)

        @blk.tensor
        def _(t, g=g):
            t.wait_ge(ac.h, ac0 + g + 1)
            for j in range(4):
                c = 4 * g + j
                t.matmul(kit.ps_sum[0:1, 0:W], ones_c[:], src(c),
                         start=(c == 0), stop=(c == FCH - 1))
                ins = t.matmul(kit.ps_sq[0:1, 0:W], ones_c[:],
                               sq[g % 2][:, j, 0:W],
                               start=(c == 0), stop=(c == FCH - 1))
                pe.inc(ins, 1)

    @blk.vector
    def _(v):
        v.wait_ge(pe.h, pe.v)

        def step(ins):
            dv.inc(ins, 1)
            v.wait_ge(dv.h, dv.v)
        step(v.tensor_scalar(strow[:, 0, 0:W], kit.ps_sum[0:1, 0:W],
                             1.0 / F, None, ALU.mult))
        step(v.tensor_scalar(strow[:, 1, 0:W], kit.ps_sq[0:1, 0:W],
                             1.0 / F, None, ALU.mult))
        step(v.tensor_tensor(strow[:, 4, 0:W], strow[:, 0, 0:W],
                             strow[:, 0, 0:W], ALU.mult))
        step(v.tensor_sub(strow[:, 1, 0:W], strow[:, 1, 0:W],
                          strow[:, 4, 0:W]))

    epsb = P.constcol(eps)
    mhalf = P.constcol(-0.5)

    @blk.scalar
    def _(q):
        q.wait_ge(dv.h, dv.v)
        ins = q.activation(strow[:, 4, 0:W], strow[:, 1, 0:W], AF.Ln,
                           bias=epsb[0:1, 0:1])
        ac.inc(ins, 1)
        q.wait_ge(ac.h, ac.v)
        ins = q.activation(strow[:, 2, 0:W], strow[:, 4, 0:W], AF.Exp,
                           scale=mhalf[0:1, 0:1])
        ac.inc(ins, 1)

    @blk.vector
    def _(v):
        v.wait_ge(ac.h, ac.v)
        ins = v.tensor_mul(strow[:, 3, 0:W], strow[:, 0, 0:W], strow[:, 2, 0:W])
        dv.inc(ins, 1)

    @blk.tensor
    def _(t):
        t.wait_ge(dv.h, dv.v)
        t.matmul(kit.ps_R[:, 0:W], ones_r[0:1, :], strow[:, 2, 0:W],
                 start=True, stop=True)
        ins = t.matmul(kit.ps_M[:, 0:W], ones_r[0:1, :], strow[:, 3, 0:W],
                       start=True, stop=True)
        pe.inc(ins, 1)

    @blk.vector
    def _(v):
        v.wait_ge(pe.h, pe.v)
        v.tensor_copy(Rb[:, 0:W], kit.ps_R[:, 0:W])
        ins = v.tensor_copy(Mb[:, 0:W], kit.ps_M[:, 0:W])
        dv.inc(ins, 1)
        v.wait_ge(dv.h, dv.v)
        flush = W <= 128

        def stepf(ins):
            dv.inc(ins, 1)
            if flush:
                v.wait_ge(dv.h, dv.v)
        if W >= 256 and srcb is not None:
            Rb3 = Rb[:, 0:W].unsqueeze(1).broadcast_to([128, 16, W])
            Mb3 = Mb[:, 0:W].unsqueeze(1).broadcast_to([128, 16, W])
            stepf(v.tensor_tensor(kit.tb[:, :, 0:W], srcb, Rb3, ALU.mult))
            stepf(v.tensor_tensor(kit.tb[:, :, 0:W], kit.tb[:, :, 0:W], Mb3,
                                  ALU.subtract))
            if gcol is None:
                if dstb is not None:
                    stepf(v.tensor_copy(dstb, kit.tb[:, :, 0:W]))
                else:
                    for c in range(FCH):
                        stepf(v.tensor_copy(dst(c), kit.tb[:, c, 0:W]))
            else:
                for c in range(FCH):
                    stepf(v.tensor_scalar(dst(c), kit.tb[:, c, 0:W],
                                          gcol(c), bcol(c),
                                          ALU.mult, ALU.add))
        else:
            for c in range(FCH):
                tmp = sq[c % 2][:, 0, 0:W]
                stepf(v.tensor_mul(tmp, src(c), Rb[:, 0:W]))
                if gcol is None:
                    stepf(v.tensor_sub(dst(c), tmp, Mb[:, 0:W]))
                else:
                    stepf(v.tensor_sub(tmp, tmp, Mb[:, 0:W]))
                    stepf(v.tensor_scalar(dst(c), tmp, gcol(c), bcol(c),
                                          ALU.mult, ALU.add))


def emit_b01(P, io):
    nc, blk = P.nc, P.block
    sl, sw, ss = P.sems["sl"], P.sems["sw"], P.sems["ss"]
    pe, dv, ac = P.sems["pe"], P.sems["dv"], P.sems["ac"]

    es = contextlib.ExitStack()
    vecs = {}
    for nm in ("ln1_g", "ln1_b", "tm_g", "tm_b", "tf", "td", "bo", "maskm"):
        vecs[nm] = P.sbuf("v_" + nm, [128, 16] if nm != "maskm" else [128, 1],
                          F32, es)
    tf1m = P.sbuf("v_tf1m", [128, 16], F32, es)
    vmix = P.sbuf("b_vmix", [128, 16], F32, es)
    vmixb = P.sbuf("b_vmixb", [128, 16], BF16, es)
    wovec = P.sbuf("b_wo", [128, 16], F32, es)
    xlp = P.sbuf("b_xlp", [128, 16], F32, es)

    @blk.vector
    def _(v):
        v.memset(xlp[:], 0.0)

    @blk.sync
    def _(q):
        for nm, t in vecs.items():
            src = io[nm]
            ins = q.dma_start(out=t[:], in_=src)
            sl.inc(ins, 16)

    @blk.vector
    def _(v):
        v.wait_ge(sl.h, sl.v)
        ins = v.tensor_scalar(tf1m[:], vecs["tf"][:], -1.0, 1.0,
                              ALU.mult, ALU.add)
        dv.inc(ins, 1)

    kit = LNKit(P, es)

    es0 = contextlib.ExitStack()
    xt = P.sbuf("b0_xt", [128, 16, 512], F32, es0)
    wtile = P.sbuf("b0_wt", [128, 16, 2048], BF16, es0)
    xlnb = P.sbuf("b0_xlnb", [128, 16, TAIL], BF16, es0)
    tpb = P.sbuf("b0_tpb", [128, TAIL], F32, es0)
    dec = P.sbuf("b0_dec", [128, TAIL], F32, es0)
    vsb = P.sbuf("b0_vsb", [128, TAIL], F32, es0)
    ps_v = P.psum("b0_psv", [128, 1024], F32, es0)
    ps_w = P.psum("b0_psw", [128, 16], F32, es0)

    @blk.sync
    def _(q):
        ins = q.dma_start(out=tpb[:], in_=io["tpos"].broadcast_to([128, TAIL]))
        sl.inc(ins, 16)

    @blk.gpsimd
    def _(g):
        for k in range(FCH):
            ins = g.dma_start(out=wtile[:, k, :],
                              in_=io["Wv"][128 * k:128 * (k + 1), :])
            sw.inc(ins, 16)

    @blk.vector
    def _(v):
        v.wait_ge(sl.h, sl.v)

    for half in range(2):
        s = 512 * half

        @blk.sync
        def _(q, s=s):
            q.wait_ge(dv.h, dv.v)
            ins = q.dma_start(
                out=xt[:],
                in_=io["xtail"][:, s:s + 512].rearrange("(c p) t -> p c t",
                                                        p=128))
            sl.inc(ins, 16)
        emit_ln(P, kit, lambda c: xt[:, c, :],
                lambda c: xt[:, c, :], 512, EPS0,
                srcb=xt[:], dstb=xt[:])
        emit_ln(P, kit, lambda c: xt[:, c, :],
                lambda c: xt[:, c, :], 512, EPSL,
                gcol=lambda c: vecs["ln1_g"][:, c:c + 1],
                bcol=lambda c: vecs["ln1_b"][:, c:c + 1],
                srcb=xt[:])
        emit_ln(P, kit, lambda c: xt[:, c, :],
                lambda c, s=s: xlnb[:, c, s:s + 512], 512, EPSL,
                gcol=lambda c: vecs["tm_g"][:, c:c + 1],
                bcol=lambda c: vecs["tm_b"][:, c:c + 1],
                srcb=xt[:])

    for hh in range(FCH):
        @blk.tensor
        def _(t, hh=hh):
            t.wait_ge(sw.h, sw.v)
            t.wait_ge(dv.h, dv.v)
            for tt in range(2):
                for k in range(FCH):
                    ins = t.matmul(ps_v[:, 512 * tt:512 * (tt + 1)],
                                   wtile[:, k, 128 * hh:128 * (hh + 1)],
                                   xlnb[:, k, 512 * tt:512 * (tt + 1)],
                                   start=(k == 0), stop=(k == FCH - 1))
                pe.inc(ins, 1)

        @blk.scalar
        def _(q, hh=hh):
            q.wait_ge(sl.h, sl.v)
            q.wait_ge(dv.h, dv.v)
            ins = q.activation(dec[:], tpb[:], AF.Exp,
                               scale=vecs["td"][:, hh:hh + 1])
            ac.inc(ins, 1)

        @blk.vector
        def _(v, hh=hh):
            v.wait_ge(pe.h, pe.v)
            v.tensor_copy(vsb[:, 0:512], ps_v[:, 0:512])
            v.tensor_copy(vsb[:, 512:1024], ps_v[:, 512:1024])
            v.wait_ge(ac.h, ac.v)
            v.tensor_mul(vsb[:], vsb[:], dec[:])
            ins = v.tensor_reduce(vmix[:, hh:hh + 1], vsb[:], AX.X, ALU.add)
            dv.inc(ins, 1)

    @blk.vector
    def _(v):
        v.wait_ge(dv.h, dv.v)
        ins = v.tensor_copy(vmixb[:], vmix[:])
        dv.inc(ins, 1)

    @blk.gpsimd
    def _(g):
        g.wait_ge(pe.h, pe.v)
        for k in range(FCH):
            ins = g.dma_start(out=wtile[:, k, :],
                              in_=io["Wo"][128 * k:128 * (k + 1), :])
            sw.inc(ins, 16)

    @blk.tensor
    def _(t):
        t.wait_ge(sw.h, sw.v)
        t.wait_ge(dv.h, dv.v)
        for m in range(FCH):
            for k in range(FCH):
                ins = t.matmul(ps_w[:, m:m + 1],
                               wtile[:, k, 128 * m:128 * (m + 1)],
                               vmixb[:, k:k + 1],
                               start=(k == 0), stop=(k == FCH - 1))
        pe.inc(ins, 1)

    @blk.vector
    def _(v):
        v.wait_ge(pe.h, pe.v)
        ins = v.tensor_add(wovec[:], ps_w[:], vecs["bo"][:])
        dv.inc(ins, 1)

    es0.close()
    P.fence()

    es1 = contextlib.ExitStack()
    x0t = P.sbuf("b1_x0", [128, 16, 256], F32, es1)
    yt = P.sbuf("b1_y", [128, 16, 256], F32, es1)
    ct = P.sbuf("b1_c", [128, 16, 256], F32, es1)
    mixb = P.sbuf("b1_mix", [128, 16, 256], BF16, es1)
    wr = P.sbuf("b1_wr", [128, 16, 2048], BF16, es1)
    rb = [P.sbuf(f"b1_r{i}", [128, 512], F32, es1) for i in range(2)]
    x2c = [P.sbuf(f"b1_x2c{i}", [128, 512], F32, es1) for i in range(2)]
    s1c = [P.sbuf(f"b1_s1c{i}", [128, 512], F32, es1) for i in range(2)]
    ps_r = P.psum("b1_psr", [128, 1024], F32, es1)

    @blk.gpsimd
    def _(g):
        for k in range(FCH):
            ins = g.dma_start(out=wr[:, k, :],
                              in_=io["Wr"][128 * k:128 * (k + 1), :])
            sw.inc(ins, 16)

    tiles = [(0, 2, True)] + [(2 + 256 * i, 256, False) for i in range(8)]
    for base, W, halo in tiles:
        @blk.sync
        def _(q, base=base, W=W):
            q.wait_ge(pe.h, pe.v)
            q.wait_ge(dv.h, dv.v)
            q.wait_ge(ac.h, ac.v)
            ins = q.dma_start(
                out=x0t[:, :, 0:W],
                in_=io["x0"][:, base:base + W].rearrange("(c p) t -> p c t",
                                                         p=128))
            sl.inc(ins, 16)

        @blk.vector
        def _(v):
            v.wait_ge(sl.h, sl.v)

        emit_ln(P, kit, lambda c, W=W: x0t[:, c, 0:W],
                lambda c, W=W: yt[:, c, 0:W], W, EPS0,
                srcb=x0t[:, :, 0:W], dstb=yt[:, :, 0:W])
        emit_ln(P, kit, lambda c, W=W: yt[:, c, 0:W],
                lambda c, W=W: ct[:, c, 0:W], W, EPSL,
                gcol=lambda c: vecs["ln1_g"][:, c:c + 1],
                bcol=lambda c: vecs["ln1_b"][:, c:c + 1],
                srcb=yt[:, :, 0:W])
        emit_ln(P, kit, lambda c, W=W: ct[:, c, 0:W],
                lambda c, W=W: ct[:, c, 0:W], W, EPSL,
                gcol=lambda c: vecs["tm_g"][:, c:c + 1],
                bcol=lambda c: vecs["tm_b"][:, c:c + 1], inplace=True,
                srcb=ct[:, :, 0:W])

        @blk.vector
        def _(v, W=W):
            for c in range(FCH):
                tmp = kit.sq[0][:, 0, 0:W]
                ins = v.tensor_scalar(tmp, ct[:, c, 0:W], vecs["tf"][:, c:c + 1],
                                      None, ALU.mult)
                dv.inc(ins, 1)
                v.wait_ge(dv.h, dv.v)
                if W > 1:
                    v.scalar_tensor_tensor(
                        mixb[:, c, 1:W], ct[:, c, 0:W - 1],
                        tf1m[:, c:c + 1], tmp[:, 1:W], ALU.mult, ALU.add)
                v.scalar_tensor_tensor(
                    mixb[:, c, 0:1], xlp[:, c:c + 1], tf1m[:, c:c + 1],
                    tmp[:, 0:1], ALU.mult, ALU.add)
            ins = v.tensor_copy(xlp[:], ct[:, :, W - 1])
            dv.inc(ins, 1)
            if halo:
                v.wait_ge(dv.h, dv.v)
                ins = v.tensor_scalar(xlp[:], xlp[:], vecs["maskm"][:, 0:1],
                                      None, ALU.mult)
                dv.inc(ins, 1)

        mix_mark = dv.v
        rmk = {"sig": [0, 0], "dve": [0, 0], "st": [0, 0]}
        if halo and "DBGxlp" in io:
            @blk.scalar
            def _(q):
                q.wait_ge(dv.h, dv.v)
                ins = q.dma_start(out=io["DBGxlp"], in_=xlp[:])
                ss.inc(ins, 16)
                ins = q.dma_start(out=io["DBGmix"],
                                  in_=mixb[:, :, 0:2])
                ss.inc(ins, 16)
        for m in range(FCH):
            par = m % 2

            @blk.tensor
            def _(t, m=m, W=W, par=par):
                t.wait_ge(dv.h, mix_mark)
                t.wait_ge(ac.h, rmk["sig"][par])
                for k in range(FCH):
                    ins = t.matmul(ps_r[:, 512 * par:512 * par + W],
                                   wr[:, k, 128 * m:128 * (m + 1)],
                                   mixb[:, k, 0:W],
                                   start=(k == 0), stop=(k == FCH - 1))
                pe.inc(ins, 1)
            pmr = pe.v

            @blk.scalar
            def _(q, m=m, W=W, par=par, pmr=pmr):
                q.wait_ge(pe.h, pmr)
                q.wait_ge(dv.h, rmk["dve"][par])
                ins = q.activation(rb[par][:, 0:W],
                                   ps_r[:, 512 * par:512 * par + W], AF.Sigmoid)
                ac.inc(ins, 1)
            rmk["sig"][par] = ac.v
            amr = ac.v

            @blk.vector
            def _(v, m=m, W=W, par=par, amr=amr):
                v.wait_ge(ac.h, amr)
                v.wait_ge(ss.h, rmk["st"][par])
                v.scalar_tensor_tensor(x2c[par][:, 0:W], rb[par][:, 0:W],
                                       wovec[:, m:m + 1], yt[:, m, 0:W],
                                       ALU.mult, ALU.add)
                ins = v.tensor_add(s1c[par][:, 0:W], x2c[par][:, 0:W],
                                   x0t[:, m, 0:W])
                dv.inc(ins, 1)
            rmk["dve"][par] = dv.v
            dmr = dv.v

            @blk.scalar
            def _(q, m=m, W=W, halo=halo, base=base, par=par, dmr=dmr):
                q.wait_ge(dv.h, dmr)
                if halo:
                    ins = q.dma_start(out=io["X2"][128 * m:128 * (m + 1), 0:1],
                                      in_=x2c[par][:, 1:2],
                                      allow_slow_non_contiguous=True)
                    ss.inc(ins, 16)
                else:
                    ins = q.dma_start(
                        out=io["X2"][128 * m:128 * (m + 1), base - 1:base - 1 + W],
                        in_=x2c[par][:, 0:W])
                    ss.inc(ins, 16)
                    ins = q.dma_start(
                        out=io["S1P"][128 * m:128 * (m + 1),
                                      base - 2:base - 2 + W],
                        in_=s1c[par][:, 0:W])
                    ss.inc(ins, 16)
                rmk["st"][par] = ss.v

    es1.close()
    es.close()
    P.fence()



def emit_proj(P, es, rhs_chunk, n_total, w_dram, w_is_f32, epilogue, out_dram,
              tag):
    blk = P.block
    sl, sw, ss = P.sems["sl"], P.sems["sw"], P.sems["ss"]
    pe, dv, ac = P.sems["pe"], P.sems["dv"], P.sems["ac"]
    npieces = n_total // 512

    wbuf = [P.sbuf(f"{tag}_w{i}", [128, 16, 512], BF16, es) for i in range(2)]
    ob = [P.sbuf(f"{tag}_ob{i}", [128, 4, T], BF16, es) for i in range(2)]
    tmp = [P.sbuf(f"{tag}_tmp{i}", [128, 512], F32, es) for i in range(2)]
    ps = P.psum(f"{tag}_ps", [128, 2048], F32, es)
    wq_marks = [0, 0]
    store_mark = [0, 0]
    bank_ep = [(None, 0)] * 4
    tmp_ep = [0, 0]

    for p in range(npieces):
        wb = wbuf[p % 2]
        obb = ob[p % 2]
        wqueue = blk.gpsimd if w_is_f32 else blk.sync
        wsem = sw if w_is_f32 else sl

        @wqueue
        def _(g, p=p, wb=wb):
            if p >= 2:
                g.wait_ge(pe.h, wq_marks[p % 2])
            for k in range(FCH):
                ins = g.dma_start(
                    out=wb[:, k, :],
                    in_=w_dram[128 * k:128 * (k + 1), 512 * p:512 * (p + 1)])
                wsem.inc(ins, 16)
        wload = wsem.v

        for ncs in range(4):
            for ts in range(4):
                bank = (ncs * 4 + ts) % 4

                @blk.tensor
                def _(t, wb=wb, ncs=ncs, ts=ts, bank=bank, wload=wload):
                    t.wait_ge((sw if w_is_f32 else sl).h, wload)
                    bs, bm = bank_ep[bank]
                    if bs is not None:
                        t.wait_ge(bs.h, bm)
                    for k in range(FCH):
                        ins = t.matmul(
                            ps[:, 512 * bank:512 * (bank + 1)],
                            wb[:, k, 128 * ncs:128 * (ncs + 1)],
                            rhs_chunk(k)[:, 512 * ts:512 * (ts + 1)],
                            start=(k == 0), stop=(k == FCH - 1))
                    pe.inc(ins, 1)
                pm = pe.v
                psv = ps[:, 512 * bank:512 * (bank + 1)]
                oslice = obb[:, ncs, 512 * ts:512 * (ts + 1)]

                if epilogue == "relu2":
                    tb = (ncs * 4 + ts) % 2

                    @blk.vector
                    def _(v, pm=pm, psv=psv, tb=tb):
                        v.wait_ge(pe.h, pm)
                        v.wait_ge(ac.h, tmp_ep[tb])
                        ins = v.tensor_scalar(tmp[tb][:], psv, 0.0, None,
                                              ALU.max)
                        dv.inc(ins, 1)
                    bank_ep[bank] = (dv, dv.v)
                    dm = dv.v

                    @blk.scalar
                    def _(q, dm=dm, oslice=oslice, tb=tb, p=p):
                        q.wait_ge(dv.h, dm)
                        q.wait_ge(ss.h, store_mark[p % 2])
                        ins = q.activation(oslice, tmp[tb][:], AF.Square)
                        ac.inc(ins, 1)
                    tmp_ep[tb] = ac.v
                else:
                    fn = AF.Sigmoid if epilogue == "sigmoid" else AF.Identity

                    @blk.scalar
                    def _(q, pm=pm, psv=psv, oslice=oslice, fn=fn, p=p):
                        q.wait_ge(pe.h, pm)
                        q.wait_ge(ss.h, store_mark[p % 2])
                        ins = q.activation(oslice, psv, fn)
                        ac.inc(ins, 1)
                    bank_ep[bank] = (ac, ac.v)
        wq_marks[p % 2] = pe.v

        @blk.sync
        def _(q, p=p, obb=obb):
            q.wait_ge(ac.h, ac.v)
            ins = q.dma_start(
                out=out_dram[512 * p:512 * (p + 1), :].rearrange(
                    "(c p2) t -> p2 c t", p2=128),
                in_=obb[:])
            ss.inc(ins, 16)
        store_mark[p % 2] = ss.v


def emit_down(P, es, a_dram, n_chunks, w_dram, w_is_f32, m_total, final, tag):
    blk = P.block
    sl, sw, ss = P.sems["sl"], P.sems["sw"], P.sems["ss"]
    pe, dv, ac = P.sems["pe"], P.sems["dv"], P.sems["ac"]
    MH = m_total // 2
    mh_ch = MH // 128

    wg = [P.sbuf(f"{tag}_wg{i}", [128, 8, MH], BF16, es) for i in range(2)]
    at = [P.sbuf(f"{tag}_at{i}", [128, 8, 512], BF16, es) for i in range(2)]
    acc = P.sbuf(f"{tag}_acc", [128, mh_ch, T], F32, es)
    ps = P.psum(f"{tag}_ps", [128, 2048], F32, es)

    groups = []
    n0 = 0
    while n0 < n_chunks:
        gn = min(8, n_chunks - n0)
        groups.append((n0, gn))
        n0 += gn
    wmarks = [0, 0]
    amarks = [0, 0]
    bank_ev = [0, 0, 0, 0]

    for half in range(2):
        for gi, (g0, gn) in enumerate(groups):
            gidx = half * len(groups) + gi
            wb = wg[gidx % 2]
            wqueue = blk.gpsimd if w_is_f32 else blk.sync
            wsem = sw if w_is_f32 else sl

            @wqueue
            def _(g, g0=g0, gn=gn, wb=wb, gidx=gidx, half=half):
                if gidx >= 2:
                    g.wait_ge(pe.h, wmarks[gidx % 2])
                for j in range(gn):
                    ins = g.dma_start(
                        out=wb[:, j, :],
                        in_=w_dram[128 * (g0 + j):128 * (g0 + j + 1),
                                   MH * half:MH * (half + 1)])
                    wsem.inc(ins, 16)

            for ts in range(4):
                tsi = gidx * 4 + ts
                ab = at[tsi % 2]

                @blk.sync
                def _(q, g0=g0, gn=gn, ab=ab, tsi=tsi, ts=ts):
                    if tsi >= 2:
                        q.wait_ge(pe.h, amarks[tsi % 2])
                    ins = q.dma_start(
                        out=ab[:, 0:gn, :],
                        in_=a_dram[128 * g0:128 * (g0 + gn),
                                   512 * ts:512 * (ts + 1)].rearrange(
                                       "(c p2) t -> p2 c t", p2=128))
                    sl.inc(ins, 16)

                for mc in range(mh_ch):
                    bank = (tsi * mh_ch + mc) % 4

                    @blk.tensor
                    def _(t, gn=gn, wb=wb, ab=ab, mc=mc, bank=bank,
                          wl=None, al=None):
                        t.wait_ge(sw.h, sw.v)
                        t.wait_ge(sl.h, sl.v)
                        bm = bank_ev[bank]
                        if bm:
                            t.wait_ge(dv.h, bm)
                        for j in range(gn):
                            ins = t.matmul(
                                ps[:, 512 * bank:512 * (bank + 1)],
                                wb[:, j, 128 * mc:128 * (mc + 1)],
                                ab[:, j, :],
                                start=(j == 0), stop=(j == gn - 1))
                        pe.inc(ins, 1)
                    pm = pe.v

                    @blk.vector
                    def _(v, gi=gi, mc=mc, ts=ts, bank=bank, pm=pm):
                        v.wait_ge(pe.h, pm)
                        dst = acc[:, mc, 512 * ts:512 * (ts + 1)]
                        srcp = ps[:, 512 * bank:512 * (bank + 1)]
                        if gi == 0:
                            ins = v.tensor_copy(dst, srcp)
                        else:
                            ins = v.tensor_add(dst, dst, srcp)
                        dv.inc(ins, 1)
                    bank_ev[bank] = dv.v
                amarks[tsi % 2] = pe.v
            wmarks[gidx % 2] = pe.v

        for mc in range(mh_ch):
            final(half * mh_ch + mc, 0, acc[:, mc, :])
        P.fence()



def emit_b2(P, io):
    nc, blk = P.nc, P.block
    sl, sw, ss = P.sems["sl"], P.sems["sw"], P.sems["ss"]
    pe, dv, ac = P.sems["pe"], P.sems["dv"], P.sems["ac"]

    es = contextlib.ExitStack()
    vecs = {}
    for nm in ("ln2_g", "ln2_b", "cm_g", "cm_b", "maskm"):
        vecs[nm] = P.sbuf("v_" + nm, [128, 16] if nm != "maskm" else [128, 1],
                          F32, es)
    xc = P.sbuf("b2_xc", [128, 16, T + 1], BF16, es)

    @blk.sync
    def _(q):
        for nm, t in vecs.items():
            ins = q.dma_start(out=t[:], in_=io[nm])
            sl.inc(ins, 16)

    es1 = contextlib.ExitStack()
    kit = LNKit(P, es1)
    x2t = P.sbuf("b2_x2t", [128, 16, 512], F32, es1)
    ut = P.sbuf("b2_ut", [128, 16, 512], F32, es1)

    tiles = [(0, 1)] + [(1 + 512 * i, 512) for i in range(4)]
    for base, W in tiles:
        @blk.sync
        def _(q, base=base, W=W):
            q.wait_ge(dv.h, dv.v)
            q.wait_ge(ac.h, ac.v)
            ins = q.dma_start(
                out=x2t[:, :, 0:W],
                in_=io["X2"][:, base:base + W].rearrange("(c p) t -> p c t",
                                                         p=128),
                allow_slow_non_contiguous=(W < 8))
            sl.inc(ins, 16)
        emit_ln(P, kit, lambda c, W=W: x2t[:, c, 0:W],
                lambda c, W=W: ut[:, c, 0:W], W, EPSL,
                gcol=lambda c: vecs["ln2_g"][:, c:c + 1],
                bcol=lambda c: vecs["ln2_b"][:, c:c + 1],
                srcb=x2t[:, :, 0:W])
        emit_ln(P, kit, lambda c, W=W: ut[:, c, 0:W],
                lambda c, W=W, base=base: xc[:, c, base:base + W], W, EPSL,
                gcol=lambda c: vecs["cm_g"][:, c:c + 1],
                bcol=lambda c: vecs["cm_b"][:, c:c + 1],
                srcb=ut[:, :, 0:W])

    @blk.vector
    def _(v):
        v.wait_ge(dv.h, dv.v)
        for c in range(FCH):
            ins = v.tensor_scalar(xc[:, c, 0:1], xc[:, c, 0:1],
                                  vecs["maskm"][:, 0:1], None, ALU.mult)
            dv.inc(ins, 1)
    es1.close()
    P.fence()

    es2 = contextlib.ExitStack()
    emit_proj(P, es2, lambda k: xc[:, k, 1:T + 1], F, io["Wcr"], True,
              "sigmoid", io["RC"], "rc")
    es2.close()
    P.fence()
    es3 = contextlib.ExitStack()
    emit_proj(P, es3, lambda k: xc[:, k, 0:T], I0, io["Wck"], True,
              "relu2", io["KC"], "kc")
    es3.close()
    es.close()
    P.fence()

    es4 = contextlib.ExitStack()
    rct = [P.sbuf(f"b2_rct{i}", [128, T], BF16, es4) for i in range(2)]
    s1t = [P.sbuf(f"b2_s1t{i}", [128, T], F32, es4) for i in range(2)]
    tmp2 = [P.sbuf(f"b2_tmp2{i}", [128, T], F32, es4) for i in range(2)]
    fmarks = [[0, 0], [0, 0]]

    def final(mc, ts, acc_ap):
        if ts != 0:
            return
        b = mc % 2

        @blk.sync
        def _(q, mc=mc, b=b):
            q.wait_ge(dv.h, fmarks[b][0])
            ins = q.dma_start(out=rct[b][:],
                              in_=io["RC"][128 * mc:128 * (mc + 1), :])
            sl.inc(ins, 16)
            ins = q.dma_start(out=s1t[b][:],
                              in_=io["S1P"][128 * mc:128 * (mc + 1), :])
            sl.inc(ins, 16)
        lm = sl.v

        @blk.vector
        def _(v, acc_ap=acc_ap, b=b, lm=lm):
            v.wait_ge(sl.h, lm)
            v.wait_ge(ss.h, fmarks[b][1])
            v.tensor_mul(tmp2[b][:], rct[b][:], acc_ap)
            ins = v.tensor_add(tmp2[b][:], tmp2[b][:], s1t[b][:])
            dv.inc(ins, 1)
        fmarks[b][0] = dv.v

        @blk.scalar
        def _(q, mc=mc, b=b):
            q.wait_ge(dv.h, fmarks[b][0])
            ins = q.dma_start(out=io["S1"][128 * mc:128 * (mc + 1), :],
                              in_=tmp2[b][:])
            ss.inc(ins, 16)
        fmarks[b][1] = ss.v

    emit_down(P, es4, io["KC"], 64, io["Wcv"], True, F, final, "vc")
    es4.close()
    P.fence()



def emit_stream_ln(P, es, src_dram, dst_sbuf, tag):
    blk = P.block
    sl, dv, ac = P.sems["sl"], P.sems["dv"], P.sems["ac"]
    esk = contextlib.ExitStack()
    kit = LNKit(P, esk)
    st = P.sbuf(f"{tag}_st", [128, 16, 512], F32, esk)
    for i in range(4):
        @blk.sync
        def _(q, i=i):
            q.wait_ge(dv.h, dv.v)
            q.wait_ge(ac.h, ac.v)
            ins = q.dma_start(
                out=st[:],
                in_=src_dram[:, 512 * i:512 * (i + 1)].rearrange(
                    "(c p) t -> p c t", p=128))
            sl.inc(ins, 16)
        emit_ln(P, kit, lambda c: st[:, c, :],
                lambda c, i=i: dst_sbuf[:, c, 512 * i:512 * (i + 1)],
                512, EPS0, srcb=st[:],
                dstb=dst_sbuf[:, :, 512 * i:512 * (i + 1)])
    esk.close()
    P.fence()


def emit_proj2(P, es, rhs_chunk, n_total, wA_dram, wB_dram, act_fn, out_dram,
               tag):
    blk = P.block
    sl, sw, ss = P.sems["sl"], P.sems["sw"], P.sems["ss"]
    pe, dv, ac = P.sems["pe"], P.sems["dv"], P.sems["ac"]
    npieces = n_total // 512

    wbufA = [P.sbuf(f"{tag}_wa{i}", [128, 16, 512], BF16, es) for i in range(2)]
    wbufB = [P.sbuf(f"{tag}_wb{i}", [128, 16, 512], BF16, es) for i in range(2)]
    ob = [P.sbuf(f"{tag}_ob{i}", [128, 4, T], BF16, es) for i in range(2)]
    tmp = [P.sbuf(f"{tag}_tmp{i}", [128, 512], F32, es) for i in range(2)]
    ps = P.psum(f"{tag}_ps", [128, 2048], F32, es)
    wq_marks = [0, 0]
    store_mark = [0, 0]
    pair_ep = [(0, 0)] * 2
    tmp_ep = [0, 0]

    for p in range(npieces):
        wa, wb = wbufA[p % 2], wbufB[p % 2]
        obb = ob[p % 2]

        @blk.sync
        def _(q, p=p, wa=wa, wb=wb):
            if p >= 2:
                q.wait_ge(pe.h, wq_marks[p % 2])
            for k in range(FCH):
                ins = q.dma_start(
                    out=wa[:, k, :],
                    in_=wA_dram[128 * k:128 * (k + 1), 512 * p:512 * (p + 1)])
                sl.inc(ins, 16)
                ins = q.dma_start(
                    out=wb[:, k, :],
                    in_=wB_dram[128 * k:128 * (k + 1), 512 * p:512 * (p + 1)])
                sl.inc(ins, 16)
        wload = sl.v

        for ncs in range(4):
            for ts in range(4):
                pair = (ncs * 4 + ts) % 2

                @blk.tensor
                def _(t, wa=wa, wb=wb, ncs=ncs, ts=ts, pair=pair, wload=wload):
                    t.wait_ge(sl.h, wload)
                    am, dm = pair_ep[pair]
                    if am:
                        t.wait_ge(ac.h, am)
                    if dm:
                        t.wait_ge(dv.h, dm)
                    for k in range(FCH):
                        ins = t.matmul(
                            ps[:, 1024 * pair:1024 * pair + 512],
                            wa[:, k, 128 * ncs:128 * (ncs + 1)],
                            rhs_chunk(k)[:, 512 * ts:512 * (ts + 1)],
                            start=(k == 0), stop=(k == FCH - 1))
                    for k in range(FCH):
                        ins2 = t.matmul(
                            ps[:, 1024 * pair + 512:1024 * pair + 1024],
                            wb[:, k, 128 * ncs:128 * (ncs + 1)],
                            rhs_chunk(k)[:, 512 * ts:512 * (ts + 1)],
                            start=(k == 0), stop=(k == FCH - 1))
                    pe.inc(ins2, 1)
                pm = pe.v
                psA = ps[:, 1024 * pair:1024 * pair + 512]
                psB = ps[:, 1024 * pair + 512:1024 * pair + 1024]
                oslice = obb[:, ncs, 512 * ts:512 * (ts + 1)]
                tb = (ncs * 4 + ts) % 2

                @blk.scalar
                def _(q, pm=pm, psA=psA, tb=tb):
                    q.wait_ge(pe.h, pm)
                    q.wait_ge(dv.h, tmp_ep[tb])
                    ins = q.activation(tmp[tb][:], psA, act_fn)
                    ac.inc(ins, 1)
                am = ac.v

                @blk.vector
                def _(v, am=am, psB=psB, oslice=oslice, tb=tb, p=p):
                    v.wait_ge(ac.h, am)
                    v.wait_ge(ss.h, store_mark[p % 2])
                    ins = v.tensor_mul(oslice, tmp[tb][:], psB)
                    dv.inc(ins, 1)
                pair_ep[pair] = (am, dv.v)
                tmp_ep[tb] = dv.v
        wq_marks[p % 2] = pe.v

        @blk.scalar
        def _(q, p=p, obb=obb):
            q.wait_ge(dv.h, dv.v)
            ins = q.dma_start(
                out=out_dram[512 * p:512 * (p + 1), :].rearrange(
                    "(c p2) t -> p2 c t", p2=128),
                in_=obb[:])
            ss.inc(ins, 16)
        store_mark[p % 2] = ss.v


def _final_stream_add(P, sin_dram, sout_dram, es):
    blk = P.block
    sl, ss, dv = P.sems["sl"], P.sems["ss"], P.sems["dv"]
    s_t = [P.sbuf(f"fs_s{i}", [128, T], F32, es) for i in range(2)]
    o_t = [P.sbuf(f"fs_o{i}", [128, T], F32, es) for i in range(2)]
    marks = [[0, 0], [0, 0]]

    def final(mc, ts, acc_ap):
        if ts != 0:
            return
        b = mc % 2

        @blk.sync
        def _(q, mc=mc, b=b):
            q.wait_ge(dv.h, marks[b][0])
            ins = q.dma_start(out=s_t[b][:],
                              in_=sin_dram[128 * mc:128 * (mc + 1), :])
            sl.inc(ins, 16)
        lm = sl.v

        @blk.vector
        def _(v, acc_ap=acc_ap, b=b, lm=lm):
            v.wait_ge(sl.h, lm)
            v.wait_ge(ss.h, marks[b][1])
            ins = v.tensor_add(o_t[b][:], s_t[b][:], acc_ap)
            dv.inc(ins, 1)
        marks[b][0] = dv.v

        @blk.scalar
        def _(q, mc=mc, b=b):
            q.wait_ge(dv.h, marks[b][0])
            ins = q.dma_start(out=sout_dram[128 * mc:128 * (mc + 1), :],
                              in_=o_t[b][:])
            ss.inc(ins, 16)
        marks[b][1] = ss.v
    return final


def emit_c(P, io):
    if io.get("_defer_gather", False):
        blk0, ccs0 = P.block, P.sems["cc"]
        ss0 = P.sems["ss"]

        @blk0.gpsimd
        def _(g):
            g.wait_ge(ss0.h, ss0.v)
            for a, b in (("wgq_s", "WGQ"), ("wuq_s", "WUQ"), ("wdq_s", "WDQ")):
                ins = g.collective_compute(
                    "AllGather", ALU.bypass, replica_groups=[list(range(NC))],
                    ins=[io[a].opt()], outs=[io[b].opt()])
                ccs0.inc(ins, 1)
                g.wait_ge(ccs0.h, ccs0.v)
    es = contextlib.ExitStack()
    y2 = P.sbuf("c_y2", [128, 16, T], BF16, es)
    emit_stream_ln(P, es, io["S1"], y2, "cln")
    es2 = contextlib.ExitStack()
    emit_proj2(P, es2, lambda k: y2[:, k, :], HID0, io["W1Q"], io["W2Q"],
               AF.Sigmoid, io["HID"], "ch")
    es2.close()
    es.close()
    P.fence()
    es3 = contextlib.ExitStack()
    fin = _final_stream_add(P, io["S1"], io["S2"], es3)
    emit_down(P, es3, io["HID"], HID0 // 128, io["W3Q"], False, F, fin, "cd")
    es3.close()
    P.fence()


def emit_e(P, io):
    es = contextlib.ExitStack()
    y4 = P.sbuf("e_y4", [128, 16, T], BF16, es)
    emit_stream_ln(P, es, io["S3"], y4, "eln")
    es2 = contextlib.ExitStack()
    emit_proj2(P, es2, lambda k: y4[:, k, :], MLP1, io["WGQ"], io["WUQ"],
               AF.Gelu_apprx_tanh, io["INTER"], "eh")
    es2.close()
    es.close()
    P.fence()
    es3 = contextlib.ExitStack()
    fin = _final_stream_add(P, io["S3"], io["OUT"], es3)
    emit_down(P, es3, io["INTER"], MLP1 // 128, io["WDQ"], False, F, fin, "ed")
    es3.close()
    P.fence()



def emit_d(P, io):
    nc, blk = P.nc, P.block
    sl, sw, ss = P.sems["sl"], P.sems["sw"], P.sems["ss"]
    pe, dv, ac = P.sems["pe"], P.sems["dv"], P.sems["ac"]
    ccs = P.sems["cc"]
    eps_r = P.constcol(EPSR)
    mhalf = P.constcol(-0.5)

    esQ = contextlib.ExitStack()
    idt = P.sbuf("d_idt", [128, 128], BF16, esQ)
    es = contextlib.ExitStack()
    y3 = P.sbuf("d_y3", [128, 16, T], BF16, es)
    emit_stream_ln(P, es, io["S2"], y3, "dln")

    es1 = contextlib.ExitStack()
    wqb = P.sbuf("d_wq", [128, 16, 2048], BF16, es1)
    wvb = P.sbuf("d_wv", [128, 16, 512], BF16, es1)
    qsq = P.sbuf("d_qsq", [128, 2048], F32, es1)
    qred = P.sbuf("d_qred", [128, 16], F32, es1)
    qtile = P.sbuf("d_qtile", [128, 2048], BF16, es1)
    vtile = P.sbuf("d_vtile", [128, 512], F32, es1)
    ps_q = P.psum("d_psq", [128, 2048], F32, es1)
    ps_v = P.psum("d_psv", [128, 512], F32, es1)

    @blk.sync
    def _(q):
        ins = q.dma_start(out=idt[:], in_=io["ident"])
        sl.inc(ins, 16)

    @blk.gpsimd
    def _(g):
        for k in range(FCH):
            ins = g.dma_start(out=wqb[:, k, :],
                              in_=io["Wq"][128 * k:128 * (k + 1), :])
            sw.inc(ins, 16)
        for k in range(FCH):
            ins = g.dma_start(out=wvb[:, k, :],
                              in_=io["Wv1"][128 * k:128 * (k + 1), :])
            sw.inc(ins, 16)

    for tc in range(FCH):
        @blk.tensor
        def _(t, tc=tc):
            t.wait_ge(sw.h, sw.v)
            t.wait_ge(dv.h, dv.v)
            t.wait_ge(ss.h, ss.v)
            for dt in range(4):
                for k in range(FCH):
                    t.matmul(ps_q[:, 512 * dt:512 * (dt + 1)],
                             y3[:, k, 128 * tc:128 * (tc + 1)],
                             wqb[:, k, 512 * dt:512 * (dt + 1)],
                             start=(k == 0), stop=(k == FCH - 1))
            for k in range(FCH):
                ins2 = t.matmul(ps_v[:], y3[:, k, 128 * tc:128 * (tc + 1)],
                                wvb[:, k, :],
                                start=(k == 0), stop=(k == FCH - 1))
            pe.inc(ins2, 1)
        pm = pe.v

        @blk.vector
        def _(v, pm=pm):
            v.wait_ge(pe.h, pm)
            ins = v.tensor_copy(vtile[:], ps_v[:])
            dv.inc(ins, 1)
        cm = dv.v

        @blk.scalar
        def _(q, tc=tc, cm=cm, pm=pm):
            q.wait_ge(dv.h, cm)
            ins = q.dma_start(out=io["V_loc"][128 * tc:128 * (tc + 1), :],
                              in_=vtile[:])
            ss.inc(ins, 16)
            q.wait_ge(pe.h, pm)
            ins = q.activation(qsq[:], ps_q[:], AF.Square)
            ac.inc(ins, 1)

        @blk.vector
        def _(v):
            v.wait_ge(ac.h, ac.v)
            ins = v.tensor_reduce(qred[:],
                                  qsq[:].rearrange("p (h d) -> p h d", h=16),
                                  AX.X, ALU.add)
            dv.inc(ins, 1)
            v.wait_ge(dv.h, dv.v)
            ins = v.tensor_scalar(qred[:], qred[:], 1.0 / 128, None, ALU.mult)
            dv.inc(ins, 1)

        @blk.scalar
        def _(q):
            q.wait_ge(dv.h, dv.v)
            ins = q.activation(qred[:], qred[:], AF.Ln, bias=eps_r[:])
            ac.inc(ins, 1)
            q.wait_ge(ac.h, ac.v)
            ins = q.activation(qred[:], qred[:], AF.Exp, scale=mhalf[:])
            ac.inc(ins, 1)

        @blk.vector
        def _(v, tc=tc):
            v.wait_ge(ac.h, ac.v)
            v.wait_ge(ss.h, ss.v)
            for hh in range(16):
                ins = v.tensor_scalar(qtile[:, 128 * hh:128 * (hh + 1)],
                                      ps_q[:, 128 * hh:128 * (hh + 1)],
                                      qred[:, hh:hh + 1], None, ALU.mult)
            dv.inc(ins, 1)
        qm = dv.v

        @blk.scalar
        def _(q, tc=tc, qm=qm):
            q.wait_ge(dv.h, qm)
            ins = q.dma_start(out=io["QBUF"][128 * tc:128 * (tc + 1), :],
                              in_=qtile[:])
            ss.inc(ins, 16)

    es1.close()
    P.fence()
    es.close()
    P.fence()

    @blk.gpsimd
    def _(g):
        ins = g.collective_compute(
            "AllGather", ALU.bypass,
            replica_groups=[[0, 1], [2, 3], [4, 5], [6, 7]],
            ins=[io["V_loc"].opt()], outs=[io["V_full"].opt()])
        ccs.inc(ins, 1)
        g.wait_ge(ccs.h, ccs.v)

    es2 = contextlib.ExitStack()
    vbf = P.sbuf("d_vbf", [128, 32, 512], BF16, es2)
    kfm = P.sbuf("d_kfm", [128, 4, 4096], BF16, es2)
    vf32 = P.sbuf("d_vf32", [128, 512], F32, es2)
    vsq = P.sbuf("d_vsq", [128, 512], F32, es2)
    kred = P.sbuf("d_kred", [128, 4], F32, es2)
    ktmb = P.sbuf("d_ktmb", [128, 512], BF16, es2)
    ps_tr = P.psum("d_pstr", [128, 1024], BF16, es2)
    vrow = 0

    @blk.gpsimd
    def _(g):
        for sc in range(32):
            ins = g.dma_start(
                out=vbf[:, sc, :],
                in_=io["V_full"][vrow + 128 * sc:vrow + 128 * (sc + 1), :])
            sw.inc(ins, 16)

    ev_mark = [0]
    for sc in range(32):
        @blk.sync
        def _(q, sc=sc):
            q.wait_ge(ccs.h, ccs.v)
            q.wait_ge(pe.h, pe.v)
            ins = q.dma_start(
                out=vf32[:],
                in_=io["V_full"][vrow + 128 * sc:vrow + 128 * (sc + 1), :])
            sl.inc(ins, 16)
        lm = sl.v

        @blk.scalar
        def _(q, lm=lm):
            q.wait_ge(sl.h, lm)
            q.wait_ge(dv.h, dv.v)
            ins = q.activation(vsq[:], vf32[:], AF.Square)
            ac.inc(ins, 1)

        @blk.vector
        def _(v):
            v.wait_ge(ac.h, ac.v)
            ins = v.tensor_reduce(kred[:],
                                  vsq[:].rearrange("p (h d) -> p h d", h=4),
                                  AX.X, ALU.add)
            dv.inc(ins, 1)
            v.wait_ge(dv.h, dv.v)
            ins = v.tensor_scalar(kred[:], kred[:], 1.0 / 128, None, ALU.mult)
            dv.inc(ins, 1)

        @blk.scalar
        def _(q):
            q.wait_ge(dv.h, dv.v)
            ins = q.activation(kred[:], kred[:], AF.Ln, bias=eps_r[:])
            ac.inc(ins, 1)
            q.wait_ge(ac.h, ac.v)
            ins = q.activation(kred[:], kred[:], AF.Exp, scale=mhalf[:])
            ac.inc(ins, 1)

        @blk.vector
        def _(v):
            v.wait_ge(ac.h, ac.v)
            for hh in range(4):
                ins = v.tensor_scalar(ktmb[:, 128 * hh:128 * (hh + 1)],
                                      vf32[:, 128 * hh:128 * (hh + 1)],
                                      kred[:, hh:hh + 1], None, ALU.mult)
            dv.inc(ins, 1)
        km = dv.v

        @blk.tensor
        def _(t, km=km):
            t.wait_ge(dv.h, km)
            t.wait_ge(sl.h, sl.v)
            t.wait_ge(dv.h, ev_mark[0])
            for hh in range(4):
                ins = t.transpose(ps_tr[:, 128 * hh:128 * (hh + 1)],
                                  ktmb[:, 128 * hh:128 * (hh + 1)], idt[:])
            pe.inc(ins, 1)
        pm = pe.v

        @blk.vector
        def _(v, sc=sc, pm=pm):
            v.wait_ge(pe.h, pm)
            for hh in range(4):
                ins = v.tensor_copy(kfm[:, hh, 128 * sc:128 * (sc + 1)],
                                    ps_tr[:, 128 * hh:128 * (hh + 1)])
            dv.inc(ins, 1)
            ev_mark[0] = dv.v

    es3 = contextlib.ExitStack()
    qbfb = P.sbuf("d_qbfb", [128, 16, 2048], BF16, es3)
    qfm = P.sbuf("d_qfm", [128, 2048], BF16, es3)
    pbuf = P.sbuf("d_pbuf", [128, 4096], BF16, es3)
    ptb = P.sbuf("d_ptb", [128, 32, 512], BF16, es3)
    mx4p = [P.sbuf(f"d_mx4{i}", [128, 8], F32, es3) for i in range(2)]
    esump = [P.sbuf(f"d_esum{i}", [128, 8], F32, es3) for i in range(2)]
    mone = P.constcol(-1.0)
    ytile = P.sbuf("d_ytile", [128, 512], BF16, es3)
    ps_s = P.psum("d_pss", [128, 2048], F32, es3)
    ps_y = P.psum("d_psy", [128, 512], F32, es3)

    @blk.sync
    def _(q):
        for tc in range(FCH):
            ins = q.dma_start(out=qbfb[:, tc, :],
                              in_=io["QBUF"][128 * tc:128 * (tc + 1), :])
            sl.inc(ins, 16)

    ptr_ev = [0]
    y_ev = [0]
    bank_mx = [0, 0]
    bank_ex = [0, 0]
    for qh in range(16):
        kv = qh // 4
        for g8 in range(2):
            @blk.tensor
            def _(t, qh=qh, g8=g8):
                t.wait_ge(sl.h, sl.v)
                t.wait_ge(dv.h, dv.v)
                for j in range(8):
                    ins = t.transpose(ps_tr[:, 128 * j:128 * (j + 1)],
                                      qbfb[:, 8 * g8 + j,
                                           128 * qh:128 * (qh + 1)], idt[:])
                pe.inc(ins, 1)
            pmq = pe.v

            @blk.vector
            def _(v, g8=g8, pmq=pmq):
                v.wait_ge(pe.h, pmq)
                ins = v.tensor_copy(qfm[:, 1024 * g8:1024 * (g8 + 1)],
                                    ps_tr[:])
                dv.inc(ins, 1)
                ptr_ev[0] = dv.v

        for tm in range(4):
            for tq in range(4):
                tc = 4 * tm + tq
                par = tc % 2
                mx4 = mx4p[par]
                esum = esump[par]
                for quar in range(4):
                    qb = quar % 2

                    @blk.tensor
                    def _(t, qh=qh, tc=tc, quar=quar, qb=qb):
                        t.wait_ge(dv.h, bank_mx[qb])
                        t.wait_ge(ac.h, bank_ex[qb])
                        for half in range(2):
                            ins = t.matmul(
                                ps_s[:, 1024 * qb + 512 * half:
                                     1024 * qb + 512 * (half + 1)],
                                qfm[:, 128 * tc:128 * (tc + 1)],
                                kfm[:, qh // 4,
                                    1024 * quar + 512 * half:
                                    1024 * quar + 512 * (half + 1)],
                                start=True, stop=True)
                        pe.inc(ins, 1)
                    pmS = pe.v

                    @blk.vector
                    def _(v, quar=quar, qb=qb, pmS=pmS, mx4=mx4):
                        v.wait_ge(pe.h, pmS)
                        ins = v.tensor_reduce(mx4[:, 4 + quar:5 + quar],
                                              ps_s[:, 1024 * qb:1024 * (qb + 1)],
                                              AX.X, ALU.max, negate=True)
                        dv.inc(ins, 1)
                    dmS = dv.v
                    bank_mx[qb] = dmS

                    @blk.scalar
                    def _(q, quar=quar, qb=qb, dmS=dmS, mx4=mx4, esum=esum):
                        q.wait_ge(dv.h, dmS)
                        ins = q.activation(
                            pbuf[:, 1024 * quar:1024 * (quar + 1)],
                            ps_s[:, 1024 * qb:1024 * (qb + 1)],
                            AF.Exp, bias=mx4[:, 4 + quar:5 + quar],
                            accum_out=esum[:, quar:quar + 1])
                        ac.inc(ins, 1)
                    bank_ex[qb] = ac.v

                @blk.vector
                def _(v, mx4=mx4):
                    v.wait_ge(ac.h, ac.v)
                    ins = v.tensor_reduce(mx4[:, 0:1], mx4[:, 4:8],
                                          AX.X, ALU.min)
                    dv.inc(ins, 1)
                    v.wait_ge(dv.h, dv.v)
                    ins = v.tensor_scalar(mx4[:, 4:8], mx4[:, 4:8],
                                          mx4[:, 0:1], None, ALU.subtract)
                    dv.inc(ins, 1)
                dmF = dv.v

                @blk.scalar
                def _(q, dmF=dmF, mx4=mx4):
                    q.wait_ge(dv.h, dmF)
                    ins = q.activation(mx4[:, 0:4], mx4[:, 4:8], AF.Exp,
                                       scale=mone[:])
                    ac.inc(ins, 1)

                @blk.vector
                def _(v, mx4=mx4, esum=esum):
                    v.wait_ge(ac.h, ac.v)

                    def step(ins):
                        dv.inc(ins, 1)
                        v.wait_ge(dv.h, dv.v)
                    step(v.tensor_mul(esum[:, 4:8], esum[:, 0:4], mx4[:, 0:4]))
                    step(v.tensor_reduce(esum[:, 0:1], esum[:, 4:8],
                                         AX.X, ALU.add))
                    step(v.reciprocal(esum[:, 1:2], esum[:, 0:1]))
                    step(v.tensor_scalar(mx4[:, 0:4], mx4[:, 0:4],
                                         esum[:, 1:2], None, ALU.mult))
                    for quar in range(4):
                        ins = v.tensor_scalar(
                            pbuf[:, 1024 * quar:1024 * (quar + 1)],
                            pbuf[:, 1024 * quar:1024 * (quar + 1)],
                            mx4[:, quar:quar + 1], None, ALU.mult)
                    dv.inc(ins, 1)
                pfix = dv.v
                if qh == 0 and tc == 0 and "DBGP" in io:
                    @blk.scalar
                    def _(q):
                        q.wait_ge(dv.h, dv.v)
                        ins = q.dma_start(out=io["DBGP"], in_=pbuf[:])
                        ss.inc(ins, 16)
                        ins = q.dma_start(out=io["DBGMX"], in_=mx4[:])
                        ss.inc(ins, 16)

                for g8 in range(4):
                    @blk.tensor
                    def _(t, g8=g8, pfix=pfix):
                        t.wait_ge(dv.h, pfix)
                        t.wait_ge(dv.h, ptr_ev[0])
                        for j in range(8):
                            ins = t.transpose(
                                ps_tr[:, 128 * j:128 * (j + 1)],
                                pbuf[:, 128 * (8 * g8 + j):
                                     128 * (8 * g8 + j + 1)], idt[:])
                        pe.inc(ins, 1)
                    pmT = pe.v

                    @blk.vector
                    def _(v, g8=g8, tq=tq, pmT=pmT):
                        v.wait_ge(pe.h, pmT)
                        for j in range(8):
                            ins = v.tensor_copy(
                                ptb[:, 8 * g8 + j, 128 * tq:128 * (tq + 1)],
                                ps_tr[:, 128 * j:128 * (j + 1)])
                        dv.inc(ins, 1)
                        ptr_ev[0] = dv.v

            @blk.tensor
            def _(t, kv=kv):
                t.wait_ge(dv.h, dv.v)
                t.wait_ge(ac.h, ac.v)
                for sc in range(32):
                    ins = t.matmul(ps_y[:], vbf[:, sc, 128 * kv:128 * (kv + 1)],
                                   ptb[:, sc, :],
                                   start=(sc == 0), stop=(sc == 31))
                pe.inc(ins, 1)
            pmY = pe.v

            @blk.scalar
            def _(q, qh=qh, tm=tm, pmY=pmY):
                q.wait_ge(pe.h, pmY)
                q.wait_ge(ss.h, y_ev[0])
                ins = q.activation(ytile[:], ps_y[:], AF.Identity)
                ac.inc(ins, 1)
                q.wait_ge(ac.h, ac.v)
                ins = q.dma_start(
                    out=io["Y"][128 * qh:128 * (qh + 1),
                                512 * tm:512 * (tm + 1)],
                    in_=ytile[:])
                ss.inc(ins, 16)
                y_ev[0] = ss.v

    es3.close()
    es2.close()
    esQ.close()
    P.fence()

    es4 = contextlib.ExitStack()
    ybuf = P.sbuf("d_ybuf", [128, 16, 2048], BF16, es4)
    wob = P.sbuf("d_wob", [128, 16, 2048], BF16, es4)
    zsq = P.sbuf("d_zsq", [128, 2048], F32, es4)
    zred = P.sbuf("d_zred", [128, 1], F32, es4)
    zbf = P.sbuf("d_zbf", [128, 2048], BF16, es4)
    afm = P.sbuf("d_afm", [128, 16, 128], BF16, es4)
    idt2 = P.sbuf("d_idt2", [128, 128], BF16, es4)
    ps_z = P.psum("d_psz", [128, 2048], F32, es4)
    ps_t2 = P.psum("d_pst2", [128, 1024], BF16, es4)

    @blk.sync
    def _(q):
        ins = q.dma_start(out=idt2[:], in_=io["ident"])
        sl.inc(ins, 16)
        for k in range(FCH):
            ins = q.dma_start(out=ybuf[:, k, :],
                              in_=io["Y"][128 * k:128 * (k + 1), :])
            sl.inc(ins, 16)

    @blk.gpsimd
    def _(g):
        for k in range(FCH):
            ins = g.dma_start(out=wob[:, k, :],
                              in_=io["Wo1"][128 * k:128 * (k + 1), :])
            sw.inc(ins, 16)

    att_ev = [0]
    for tc in range(FCH):
        @blk.tensor
        def _(t, tc=tc):
            t.wait_ge(sl.h, sl.v)
            t.wait_ge(sw.h, sw.v)
            t.wait_ge(ac.h, ac.v)
            t.wait_ge(dv.h, dv.v)
            for mt in range(4):
                for k in range(FCH):
                    ins = t.matmul(ps_z[:, 512 * mt:512 * (mt + 1)],
                                   ybuf[:, k, 128 * tc:128 * (tc + 1)],
                                   wob[:, k, 512 * mt:512 * (mt + 1)],
                                   start=(k == 0), stop=(k == FCH - 1))
            pe.inc(ins, 1)
        pmZ = pe.v

        @blk.scalar
        def _(q, pmZ=pmZ):
            q.wait_ge(pe.h, pmZ)
            q.wait_ge(dv.h, dv.v)
            ins = q.activation(zsq[:], ps_z[:], AF.Square)
            ac.inc(ins, 1)

        @blk.vector
        def _(v):
            v.wait_ge(ac.h, ac.v)
            ins = v.tensor_reduce(zred[:], zsq[:], AX.X, ALU.add)
            dv.inc(ins, 1)
            v.wait_ge(dv.h, dv.v)
            ins = v.tensor_scalar(zred[:], zred[:], 1.0 / F, None, ALU.mult)
            dv.inc(ins, 1)

        @blk.scalar
        def _(q):
            q.wait_ge(dv.h, dv.v)
            ins = q.activation(zred[:], zred[:], AF.Ln, bias=eps_r[:])
            ac.inc(ins, 1)
            q.wait_ge(ac.h, ac.v)
            ins = q.activation(zred[:], zred[:], AF.Exp, scale=mhalf[:])
            ac.inc(ins, 1)

        @blk.vector
        def _(v):
            v.wait_ge(ac.h, ac.v)
            ins = v.tensor_scalar(zbf[:], ps_z[:], zred[:, 0:1], None, ALU.mult)
            dv.inc(ins, 1)
        zm = dv.v

        for g8 in range(2):
            @blk.tensor
            def _(t, zm=zm, g8=g8):
                t.wait_ge(dv.h, zm)
                t.wait_ge(dv.h, dv.v)
                t.wait_ge(ss.h, att_ev[0])
                for j in range(8):
                    ins = t.transpose(ps_t2[:, 128 * j:128 * (j + 1)],
                                      zbf[:, 128 * (8 * g8 + j):
                                          128 * (8 * g8 + j + 1)], idt2[:])
                pe.inc(ins, 1)
            pmT = pe.v

            @blk.vector
            def _(v, g8=g8, pmT=pmT):
                v.wait_ge(pe.h, pmT)
                for j in range(8):
                    ins = v.tensor_copy(afm[:, 8 * g8 + j, :],
                                        ps_t2[:, 128 * j:128 * (j + 1)])
                dv.inc(ins, 1)

        @blk.scalar
        def _(q, tc=tc):
            q.wait_ge(dv.h, dv.v)
            ins = q.dma_start(
                out=io["ATT"][:, 128 * tc:128 * (tc + 1)].rearrange(
                    "(c p) t -> p c t", p=128),
                in_=afm[:])
            ss.inc(ins, 16)
            att_ev[0] = ss.v

    es4.close()
    P.fence()

    es5 = contextlib.ExitStack()
    s2t = P.sbuf("d_s2t", [128, 512], F32, es5)
    att_t = P.sbuf("d_attt", [128, 512], BF16, es5)
    s3t = P.sbuf("d_s3t", [128, 512], F32, es5)
    for mc in range(FCH):
        for ts in range(4):
            @blk.sync
            def _(q, mc=mc, ts=ts):
                q.wait_ge(dv.h, dv.v)
                ins = q.dma_start(out=s2t[:],
                                  in_=io["S2"][128 * mc:128 * (mc + 1),
                                               512 * ts:512 * (ts + 1)])
                sl.inc(ins, 16)
                ins = q.dma_start(out=att_t[:],
                                  in_=io["ATT"][128 * mc:128 * (mc + 1),
                                                512 * ts:512 * (ts + 1)])
                sl.inc(ins, 16)

            @blk.vector
            def _(v):
                v.wait_ge(sl.h, sl.v)
                v.wait_ge(ss.h, ss.v)
                ins = v.tensor_add(s3t[:], s2t[:], att_t[:])
                dv.inc(ins, 1)

            @blk.scalar
            def _(q, mc=mc, ts=ts):
                q.wait_ge(dv.h, dv.v)
                ins = q.dma_start(out=io["S3"][128 * mc:128 * (mc + 1),
                                               512 * ts:512 * (ts + 1)],
                                  in_=s3t[:])
                ss.inc(ins, 16)
    es5.close()
    P.fence()



def _run(P, in_maps):
    res = run_bass_kernel_spmd(P.nc, in_maps, core_ids=list(range(NC)))
    P.close()
    return res.results


def _build_prep():
    P = Prog()
    io = {}
    for n, (s, d) in {
        "w1T_s": ([256, HID0], F32), "w2T_s": ([256, HID0], F32),
        "w3T_s": ([704, F], F32),
        "wgT_s": ([128, MLP1], U8), "wuT_s": ([128, MLP1], U8),
        "wdT_s": ([1024, F], U8),
        "gsT_s": ([8, MLP1], F32), "usT_s": ([8, MLP1], F32),
        "dsT_s": ([64, F], F32),
    }.items():
        io[n] = P.dram(n, s, d, kind="ExternalInput")
    for n, s in (("gpart", [1, 4]), ("gall", [8, 4]), ("ginv", [1, 4])):
        io[n] = P.dram(n, s, F32)
    for n, s in {
        "w1q_s": [256, HID0], "w2q_s": [256, HID0], "w3q_s": [704, F],
        "wgq_s": [256, MLP1], "wuq_s": [256, MLP1], "wdq_s": [2048, F],
    }.items():
        io[n] = P.dram(n, s, BF16, kind="ExternalOutput")
    P.start_block()
    emit_prep(P, io, gather=False)

    blk, ss = P.block, P.sems["ss"]

    @blk.gpsimd
    def _(g):
        for s in P.sems.values():
            if s.v:
                g.wait_ge(s.h, s.v)
    return P


def _build_b01():
    P = Prog()
    io = {}
    io["x0"] = P.dram("x0", [F, 2050], F32, kind="ExternalInput")
    io["xtail"] = P.dram("xtail", [F, 1024], F32, kind="ExternalInput")
    io["tpos"] = P.dram("tpos", [1, 1024], F32, kind="ExternalInput")
    io["maskm"] = P.dram("maskm", [128, 1], F32, kind="ExternalInput")
    for nm in ("Wv", "Wr", "Wo"):
        io[nm] = P.dram(nm, [F, F], F32, kind="ExternalInput")
    for nm in ("ln1_g", "ln1_b", "tm_g", "tm_b", "tf", "td", "bo"):
        io[nm] = P.dram(nm, [128, 16], F32, kind="ExternalInput")
    io["X2"] = P.dram("X2", [F, T + 1], F32, kind="ExternalOutput")
    io["S1P"] = P.dram("S1P", [F, T], F32, kind="ExternalOutput")
    P.start_block()
    emit_b01(P, io)
    return P


def _build_b2():
    P = Prog()
    io = {}
    io["X2"] = P.dram("X2", [F, T + 1], F32, kind="ExternalInput")
    io["S1P"] = P.dram("S1P", [F, T], F32, kind="ExternalInput")
    io["maskm"] = P.dram("maskm", [128, 1], F32, kind="ExternalInput")
    for nm in ("Wcr",):
        io[nm] = P.dram(nm, [F, F], F32, kind="ExternalInput")
    io["Wck"] = P.dram("Wck", [F, I0], F32, kind="ExternalInput")
    io["Wcv"] = P.dram("Wcv", [I0, F], F32, kind="ExternalInput")
    for nm in ("ln2_g", "ln2_b", "cm_g", "cm_b"):
        io[nm] = P.dram(nm, [128, 16], F32, kind="ExternalInput")
    io["KC"] = P.dram("KC", [I0, T], BF16)
    io["RC"] = P.dram("RC", [F, T], BF16)
    io["S1"] = P.dram("S1", [F, T], F32, kind="ExternalOutput")
    P.start_block()
    emit_b2(P, io)
    return P


def _build_c():
    P = Prog()
    io = {}
    io["S1"] = P.dram("S1", [F, T], F32, kind="ExternalInput")
    io["W1Q"] = P.dram("W1Q", [F, HID0], BF16, kind="ExternalInput")
    io["W2Q"] = P.dram("W2Q", [F, HID0], BF16, kind="ExternalInput")
    io["W3Q"] = P.dram("W3Q", [HID0, F], BF16, kind="ExternalInput")
    io["HID"] = P.dram("HID", [HID0, T], BF16)
    io["S2"] = P.dram("S2", [F, T], F32, kind="ExternalOutput")
    P.start_block()
    emit_c(P, io)
    return P


def _build_d():
    P = Prog()
    io = {}
    io["S2"] = P.dram("S2", [F, T], F32, kind="ExternalInput")
    io["Wq"] = P.dram("Wq", [F, F], F32, kind="ExternalInput")
    io["Wv1"] = P.dram("Wv1", [F, 512], F32, kind="ExternalInput")
    io["Wo1"] = P.dram("Wo1", [F, F], F32, kind="ExternalInput")
    io["ident"] = P.dram("ident", [128, 128], BF16, kind="ExternalInput")
    io["V_loc"] = P.dram("V_loc", [T, 512], F32)
    io["V_full"] = P.dram("V_full", [2 * T, 512], F32)
    io["QBUF"] = P.dram("QBUF", [T, F], BF16)
    io["Y"] = P.dram("Y", [F, T], BF16)
    io["ATT"] = P.dram("ATT", [F, T], BF16)
    io["S3"] = P.dram("S3", [F, T], F32, kind="ExternalOutput")
    P.start_block()
    emit_d(P, io)
    return P


def _build_e():
    P = Prog()
    io = {}
    io["S3"] = P.dram("S3", [F, T], F32, kind="ExternalInput")
    io["WGQ"] = P.dram("WGQ", [F, MLP1], BF16, kind="ExternalInput")
    io["WUQ"] = P.dram("WUQ", [F, MLP1], BF16, kind="ExternalInput")
    io["WDQ"] = P.dram("WDQ", [MLP1, F], BF16, kind="ExternalInput")
    io["INTER"] = P.dram("INTER", [MLP1, T], BF16)
    io["OUT"] = P.dram("OUT", [F, T], F32, kind="ExternalOutput")
    P.start_block()
    emit_e(P, io)
    return P


def _build_fused():
    P = Prog()
    io = {}
    for n, (s, d) in {
        "w1T_s": ([256, HID0], F32), "w2T_s": ([256, HID0], F32),
        "w3T_s": ([704, F], F32),
        "wgT_s": ([128, MLP1], U8), "wuT_s": ([128, MLP1], U8),
        "wdT_s": ([1024, F], U8),
        "gsT_s": ([8, MLP1], F32), "usT_s": ([8, MLP1], F32),
        "dsT_s": ([64, F], F32),
        "x0": ([F, 2050], F32), "xtail": ([F, 1024], F32),
        "tpos": ([1, 1024], F32), "maskm": ([128, 1], F32),
        "Wv": ([F, F], F32), "Wr": ([F, F], F32), "Wo": ([F, F], F32),
        "Wcr": ([F, F], F32), "Wck": ([F, I0], F32), "Wcv": ([I0, F], F32),
        "Wq": ([F, F], F32), "Wv1": ([F, 512], F32), "Wo1": ([F, F], F32),
        "ident": ([128, 128], BF16),
    }.items():
        io[n] = P.dram(n, s, d, kind="ExternalInput")
    for nm in ("ln1_g", "ln1_b", "tm_g", "tm_b", "tf", "td", "bo",
               "ln2_g", "ln2_b", "cm_g", "cm_b"):
        io[nm] = P.dram(nm, [128, 16], F32, kind="ExternalInput")
    for n, (s, d) in {
        "gpart": ([1, 4], F32), "gall": ([8, 4], F32), "ginv": ([1, 4], F32),
        "w1q_s": ([256, HID0], BF16), "w2q_s": ([256, HID0], BF16),
        "w3q_s": ([704, F], BF16), "wgq_s": ([256, MLP1], BF16),
        "wuq_s": ([256, MLP1], BF16), "wdq_s": ([2048, F], BF16),
        "W1Q": ([F, HID0], BF16), "W2Q": ([F, HID0], BF16),
        "W3Q": ([HID0, F], BF16), "WGQ": ([F, MLP1], BF16),
        "WUQ": ([F, MLP1], BF16), "WDQ": ([MLP1, F], BF16),
        "X2": ([F, T + 1], F32), "S1P": ([F, T], F32),
        "KC": ([I0, T], BF16), "RC": ([F, T], BF16), "S1": ([F, T], F32),
        "HID": ([HID0, T], BF16), "S2": ([F, T], F32),
        "V_loc": ([T, 512], F32), "V_full": ([2 * T, 512], F32),
        "QBUF": ([T, F], BF16), "Y": ([F, T], BF16), "ATT": ([F, T], BF16),
        "S3": ([F, T], F32), "INTER": ([MLP1, T], BF16),
    }.items():
        io[n] = P.dram(n, s, d)
    io["OUT"] = P.dram("OUT", [F, T], F32, kind="ExternalOutput")

    io["_defer_gather"] = True
    P.start_block()
    emit_prep(P, io, gather=True)
    emit_b01(P, io)
    emit_b2(P, io)
    emit_c(P, io)
    emit_d(P, io)
    emit_e(P, io)
    return P


FUSED = True
IN_KEYS = ["w1T_s", "w2T_s", "w3T_s", "wgT_s", "wuT_s", "wdT_s",
           "gsT_s", "usT_s", "dsT_s", "x0", "xtail", "tpos", "maskm",
           "Wv", "Wr", "Wo", "Wcr", "Wck", "Wcv", "Wq", "Wv1", "Wo1",
           "ident", "ln1_g", "ln1_b", "tm_g", "tm_b", "tf", "td", "bo",
           "ln2_g", "ln2_b", "cm_g", "cm_b"]


def kernel_fused(inputs):
    per = stage_inputs(inputs)
    r = _run(_build_fused(), [{k: m[k] for k in IN_KEYS} for m in per])
    return assemble_output([r[c]["OUT"] for c in range(NC)])


def kernel(**inputs):
    if FUSED:
        return kernel_fused(inputs)
    return kernel_unfused(inputs)


def kernel_unfused(inputs):
    per = stage_inputs(inputs)
    cat = np.concatenate

    keys = ["w1T_s", "w2T_s", "w3T_s", "wgT_s", "wuT_s", "wdT_s",
            "gsT_s", "usT_s", "dsT_s"]
    r1 = _run(_build_prep(), [{k: m[k] for k in keys} for m in per])
    W1Q = cat([r1[c]["w1q_s"] for c in range(NC)], 0)
    W2Q = cat([r1[c]["w2q_s"] for c in range(NC)], 0)
    W3Q = cat([r1[c]["w3q_s"] for c in range(NC)], 0)
    WGQ = cat([r1[c]["wgq_s"] for c in range(NC)], 0)
    WUQ = cat([r1[c]["wuq_s"] for c in range(NC)], 0)
    WDQ = cat([r1[c]["wdq_s"] for c in range(NC)], 0)

    keys = ["x0", "xtail", "tpos", "maskm", "Wv", "Wr", "Wo",
            "ln1_g", "ln1_b", "tm_g", "tm_b", "tf", "td", "bo"]
    r2 = _run(_build_b01(), [{k: m[k] for k in keys} for m in per])

    keys = ["maskm", "Wcr", "Wck", "Wcv", "ln2_g", "ln2_b", "cm_g", "cm_b"]
    r3 = _run(_build_b2(), [
        dict({k: m[k] for k in keys}, X2=r2[c]["X2"], S1P=r2[c]["S1P"])
        for c, m in enumerate(per)])

    r4 = _run(_build_c(), [
        {"S1": r3[c]["S1"], "W1Q": W1Q, "W2Q": W2Q, "W3Q": W3Q}
        for c in range(NC)])

    keys = ["Wq", "Wv1", "Wo1", "ident"]
    r5 = _run(_build_d(), [
        dict({k: m[k] for k in keys}, S2=r4[c]["S2"])
        for c, m in enumerate(per)])

    r6 = _run(_build_e(), [
        {"S3": r5[c]["S3"], "WGQ": WGQ, "WUQ": WUQ, "WDQ": WDQ}
        for c in range(NC)])

    return assemble_output([r6[c]["OUT"] for c in range(NC)])
